# version 22
# baseline (speedup 1.0000x reference)
import numpy as np
from contextlib import ExitStack

import concourse.bass as bass
import concourse.mybir as mybir
from concourse.bass_utils import run_bass_kernel_spmd

F32 = mybir.dt.float32
BF16 = mybir.dt.bfloat16
I32 = mybir.dt.int32
Alu = mybir.AluOpType
Act = mybir.ActivationFunctionType
Axis = mybir.AxisListType
Pool = mybir.EngineType.Pool

NCORES = 8
DM = 1024
NH = 16
DK = 64
B = 8
S = 2048
N = B * S
HID = 3072
NCH = DM // 128
SC = float(np.float32(1.0) / np.float32(127.0))

P1T = 512
NP1 = N // P1T
QB = 256
NQB = S // QB
KT = 128
NKT = S // KT
NIT = NQB * NKT
CH = S
NT4 = CH // 128
FBLK = 512
NFB = CH // FBLK
NHC = HID // 128
EPS = 1e-5

DEBUG = False


def _slab(t, pp):
    return t[:, pp * 4096:(pp + 1) * 4096]


def _slab_h(t, pp):
    return t[:, :].rearrange("p (a h c) -> p a h c", a=2, h=2, c=2048)[:, pp]


def _slab_hgbc(t, pp):
    return t[:, :].rearrange("p (a h gb c) -> p a h gb c", a=2, h=2, gb=8, c=256)[:, pp]


def _v_sched():
    ops = []

    def maxsub(j):
        return [("max1", j), ("max2", j), ("max3", j), ("sub", j)]

    ops += maxsub(0)
    for it in range(NIT):
        if it + 1 < NIT:
            ops += maxsub(it + 1)
        ops += [("sum1", it), ("sum2", it), ("zl3", it), ("mul", it)]
        if it % NKT == NKT - 1:
            ops.append(("evac", it // NKT))
    return {op: i + 1 for i, op in enumerate(ops)}, len(ops)


def _s_sched():
    ops = []
    for j in (0, 1):
        for g in range(4):
            ops.append(("cp", j, g))
    for it in range(NIT):
        ops.append(("exp", it))
        if it + 2 < NIT:
            for g in range(4):
                ops.append(("cp", it + 2, g))
        ops.append(("ln", it))
        ops.append(("rexp", it))
    return {op: i + 1 for i, op in enumerate(ops)}, len(ops)


VCP, V_TOTAL = _v_sched()
SCP, S_TOTAL = _s_sched()


def build_nc(debug=None):
    if debug is None:
        debug = DEBUG
    nc = bass.Bass("TRN2", target_bir_lowering=False, debug=False)

    x_hbm = nc.dram_tensor("xfull", [N, DM], F32, kind="ExternalInput")
    xch_hbm = nc.dram_tensor("xch", [CH, DM], F32, kind="ExternalInput")
    wq_hbm = nc.dram_tensor("wq_s", [128, DM], F32, kind="ExternalInput")
    wk_hbm = nc.dram_tensor("wk_s", [128, DM], F32, kind="ExternalInput")
    wv_hbm = nc.dram_tensor("wv_s", [128, DM], F32, kind="ExternalInput")
    bq_hbm = nc.dram_tensor("bq_s", [128, 1], F32, kind="ExternalInput")
    bk_hbm = nc.dram_tensor("bk_s", [128, 1], F32, kind="ExternalInput")
    bv_hbm = nc.dram_tensor("bv_s", [128], F32, kind="ExternalInput")
    wo_hbm = nc.dram_tensor("wo", [DM, DM], F32, kind="ExternalInput")
    bo_hbm = nc.dram_tensor("bo", [DM], F32, kind="ExternalInput")
    wg_hbm = nc.dram_tensor("wgate", [HID, DM], I32, kind="ExternalInput")
    wu_hbm = nc.dram_tensor("wup", [HID, DM], I32, kind="ExternalInput")
    wd_hbm = nc.dram_tensor("wdown", [DM, HID], I32, kind="ExternalInput")
    out_hbm = nc.dram_tensor("out", [CH, DM], F32, kind="ExternalOutput")

    import ml_dtypes
    ident_dram = nc.inline_tensor(np.eye(128, dtype=ml_dtypes.bfloat16), name="ident")
    cc_in = nc.dram_tensor("cc_in", [2, 128, N // 2], BF16)
    wgT_dram = nc.dram_tensor("wgT_dram", [NCH, 128, HID], BF16)
    wuT_dram = nc.dram_tensor("wuT_dram", [NCH, 128, HID], BF16)
    wdT_dram = nc.dram_tensor("wdT_dram", [NHC, 128, DM], BF16)
    woT_dram = nc.dram_tensor("woT_dram", [NCH, 128, DM], BF16)
    cc_out = nc.dram_tensor("cc_out", [2, NCORES, 128, N // 2], BF16, addr_space="Shared")

    dbg = {}
    if debug:
        dbg["q"] = nc.dram_tensor("dbg_q", [128, N], F32, kind="ExternalOutput")
        dbg["k"] = nc.dram_tensor("dbg_k", [128, N], F32, kind="ExternalOutput")
        dbg["v"] = nc.dram_tensor("dbg_v", [128, N], F32, kind="ExternalOutput")
        dbg["heads"] = nc.dram_tensor("dbg_heads", [128, N], F32, kind="ExternalOutput")
        dbg["r1n"] = nc.dram_tensor("dbg_r1n", [128, NT4 * 1024], F32, kind="ExternalOutput")

    ctx = ExitStack()
    with ctx:
        E = ctx.enter_context

        bq_sb = E(nc.sbuf_tensor("bq_sb", [128, 1], F32))
        bk_sb = E(nc.sbuf_tensor("bk_sb", [128, 1], F32))
        bvb = E(nc.sbuf_tensor("bvb", [128, 128], F32))
        bob = E(nc.sbuf_tensor("bob", [128, 1024], F32))
        epsv = E(nc.sbuf_tensor("epsv", [128, 1], F32))
        idb = E(nc.sbuf_tensor("idb", [128, 128], BF16))

        def sem(name):
            return E(nc.semaphore(name))

        p1_x = [sem("p1_x0"), sem("p1_x1")]
        p1_w = sem("p1_w")
        p1_bc = sem("p1_bc")
        p1_b = sem("p1_b")
        misc = sem("misc")
        p1_id = sem("p1_id")
        p1_tp = [sem("p1_tp0"), sem("p1_tp1")]
        p1_te = [sem("p1_te0"), sem("p1_te1")]
        p1_wt = sem("p1_wt")
        p1_mmq = [sem("p1_mmq0"), sem("p1_mmq1")]
        p1_mmk = [sem("p1_mmk0"), sem("p1_mmk1")]
        p1_mmv = [sem("p1_mmv0"), sem("p1_mmv1")]
        p1_pe = [sem("p1_pe0"), sem("p1_pe1")]
        p1_evq = [sem("p1_evq0"), sem("p1_evq1")]
        p1_evk = [sem("p1_evk0"), sem("p1_evk1")]
        p1_evv = [sem("p1_evv0"), sem("p1_evv1")]
        p2_mmA = sem("p2_mmA")
        p2_mmB = sem("p2_mmB")
        vch = sem("vch")
        gch = sem("gch")
        sch = sem("sch")
        p2_av = sem("p2_av")
        p3_ccin = sem("p3_ccin")
        p3_ci1 = sem("p3_ci1")
        p3_ag = sem("p3_ag")
        p3_hm0 = sem("p3_hm0")
        p3_hm1 = sem("p3_hm1")
        p3_wol = sem("p3_wol")
        p4_xch = [sem("p4_xch0"), sem("p4_xch1")]
        p4_mm = [sem("p4_mm0"), sem("p4_mm1")]
        p4_r1 = [sem("p4_r10"), sem("p4_r11")]
        p4_var = [sem("p4_var0"), sem("p4_var1")]
        p4_sd = [sem("p4_sd0"), sem("p4_sd1")]
        p4_fin = sem("p4_fin")
        p4_sq = sem("p4_sq")
        p4_jfree = sem("p4_jfree")
        p5_sq = sem("p5_sq")
        p5_jfree = sem("p5_jfree")
        p5_cast = [sem("p5_cast0"), sem("p5_cast1")]
        p5_rtx = [sem("p5_rtx0"), sem("p5_rtx1")]
        p5_wg = [sem("p5_wg0"), sem("p5_wg1")]
        pw_l = sem("pw_l")
        pw_x = sem("pw_x")
        pw_s = sem("pw_s")
        p5_gu = sem("p5_gu")
        p5_silu = [sem("p5_silu0"), sem("p5_silu1")]
        p5_upev = [sem("p5_upev0"), sem("p5_upev1")]
        p5_h = sem("p5_h")
        p5_wdl = sem("p5_wdl")
        p5_wdx = sem("p5_wdx")
        p5_dn = [sem("p5_dn0"), sem("p5_dn1")]
        p5_dnev = [sem("p5_dnev0"), sem("p5_dnev1")]
        p5_var = [sem("p5_var0"), sem("p5_var1")]
        p5_sd = [sem("p5_sd0"), sem("p5_sd1")]
        p5_ln = [sem("p5_ln0"), sem("p5_ln1")]
        p5_out = [sem("p5_out0"), sem("p5_out1")]
        dbg_sem = sem("dbg") if debug else None

        ctxA = E(ExitStack())
        qc = ctxA.enter_context(nc.sbuf_tensor("qc", [128, N], BF16))
        kc = ctxA.enter_context(nc.sbuf_tensor("kc", [128, N], BF16))
        vc = ctxA.enter_context(nc.sbuf_tensor("vc", [128, N], BF16))
        heads = ctxA.enter_context(nc.sbuf_tensor("heads", [128, N], BF16))
        ctxB1 = ExitStack()
        wstage = ctxB1.enter_context(nc.sbuf_tensor("wstage", [128, 3 * 1024], BF16))
        wqT = ctxB1.enter_context(nc.sbuf_tensor("wqT", [128, 1024], BF16))
        wkT = ctxB1.enter_context(nc.sbuf_tensor("wkT", [128, 1024], BF16))
        wvT = ctxB1.enter_context(nc.sbuf_tensor("wvT", [128, 1024], BF16))
        xbt = [ctxB1.enter_context(nc.sbuf_tensor(f"xbt{p}", [128, 4096], BF16)) for p in range(2)]
        xT = [ctxB1.enter_context(nc.sbuf_tensor(f"xT{p}", [128, 4096], BF16)) for p in range(2)]
        ctxB1.close()
        sc_sb = ctxA.enter_context(nc.sbuf_tensor("sc_sb", [128, 8192], BF16))
        e_sb = ctxA.enter_context(nc.sbuf_tensor("e_sb", [128, 8192], BF16))
        e2_sb = ctxA.enter_context(nc.sbuf_tensor("e2_sb", [128, 8192], BF16))
        mtmp = ctxA.enter_context(nc.sbuf_tensor("mtmp", [128, 4096], BF16))
        zbuf = ctxA.enter_context(nc.sbuf_tensor("zbuf", [128, 1024], F32))
        pwin = ctxA.enter_context(nc.sbuf_tensor("pwin", [128, 3072], BF16))
        pwout = ctxA.enter_context(nc.sbuf_tensor("pwout", [128, 3072], BF16))
        rbuf = ctxA.enter_context(nc.sbuf_tensor("rbuf", [128, 1024], BF16))
        psA = E(ExitStack())
        q_ps = [psA.enter_context(nc.psum_tensor(f"q_ps{p}", [128, 512], F32)) for p in range(2)]
        k_ps = [psA.enter_context(nc.psum_tensor(f"k_ps{p}", [128, 512], F32)) for p in range(2)]
        v_ps = [psA.enter_context(nc.psum_tensor(f"v_ps{p}", [128, 512], F32)) for p in range(2)]

        psA_close = psA


        ctx.callback(lambda: None)

        psA_close.close()
        psC = E(ExitStack())
        scoresA = psC.enter_context(nc.psum_tensor("scoresA", [128, 1024], F32))
        scoresB = psC.enter_context(nc.psum_tensor("scoresB", [128, 1024], F32))
        heads_ps = psC.enter_context(nc.psum_tensor("heads_ps", [128, 2048], F32))
        q_ps = [scoresA[:, 0:512], scoresA[:, 512:1024]]
        k_ps = [scoresB[:, 0:512], scoresB[:, 512:1024]]
        v_ps = [heads_ps[:, 0:512], heads_ps[:, 512:1024]]

        ctxA.close()
        r1n = E(nc.sbuf_tensor("r1n", [128, NT4 * 1024], F32))
        xch_sb = [E(nc.sbuf_tensor(f"xch_sb{p}", [128, 1024], F32)) for p in range(2)]
        r1st = [E(nc.sbuf_tensor(f"r1st{p}", [128, 1024], F32)) for p in range(2)]
        junk = E(nc.sbuf_tensor("junk", [128, 1024], F32))
        lnsc = [E(nc.sbuf_tensor(f"lnsc{p}", [128, 8], F32)) for p in range(2)]
        castst = [E(nc.sbuf_tensor(f"castst{p}", [128, 1024], BF16)) for p in range(2)]
        out_sb = [E(nc.sbuf_tensor(f"out_sb{p}", [128, 1024], F32)) for p in range(2)]
        ctxE1 = E(ExitStack())
        heads_my = ctxE1.enter_context(nc.sbuf_tensor("heads_my", [128, N], BF16))
        del heads_my
        woT = ctxE1.enter_context(nc.sbuf_tensor("woT", [128, 8192], BF16))
        wostage = ctxE1.enter_context(nc.sbuf_tensor("wostage", [128, 1024], BF16))
        ctxE1.close()
        wdT = E(nc.sbuf_tensor("wdT", [128, NHC * 1024], BF16))
        r1nT = E(nc.sbuf_tensor("r1nT", [128, 8192], BF16))
        hfull = E(nc.sbuf_tensor("hfull", [128, NHC * FBLK], BF16))
        wguT = [E(nc.sbuf_tensor(f"wguT{p}", [128, 2048], BF16)) for p in range(2)]
        silu_st = [E(nc.sbuf_tensor(f"silu_st{p}", [128, FBLK], BF16)) for p in range(2)]
        upbf = [E(nc.sbuf_tensor(f"upbf{p}", [128, FBLK], BF16)) for p in range(2)]
        wo_ps = [scoresA, scoresB]
        tp_bank = [heads_ps[:, 1024:1536].bitcast(BF16),
                   heads_ps[:, 1536:2048].bitcast(BF16)]
        gate_ps = [heads_ps[:, 0:512], heads_ps[:, 512:1024]]
        up_ps = [heads_ps[:, 1024:1536], heads_ps[:, 1536:2048]]
        down_ps = [scoresA, scoresB]

        pw_jobs = []
        for hc in range(NHC):
            pw_jobs.append(("wg", hc))
        for hc in range(NHC):
            pw_jobs.append(("wu", hc))
        for dch in range(NCH):
            pw_jobs.append(("wd", dch))
        for dch in range(NCH):
            pw_jobs.append(("wo", dch))
        PW_NX = {"wg": NCH, "wu": NCH, "wd": NHC, "wo": NCH}
        pw_xtot = [0]
        _pwx_before = {}
        for j, job in enumerate(pw_jobs):
            _pwx_before[j] = pw_xtot[0]
            pw_xtot[0] += PW_NX[job[0]]

        block = E(nc.Block())

        LNC_NM, LNC_S2, LNC_T1, LNC_M2, LNC_VAR, LNC_SD = range(6)

        def ln_stats(v, p, src, varsem, sqsem, sq_val, jfree):
            sc = lnsc[p]
            v.reduce_sum(sc[:, LNC_T1:LNC_T1 + 1], src[:, :], axis=Axis.X)
            v.drain()
            v.tensor_scalar_mul(sc[:, LNC_NM:LNC_NM + 1], sc[:, LNC_T1:LNC_T1 + 1],
                                -1.0 / DM)
            v.drain()
            v.wait_ge(sqsem, sq_val)
            v.reduce_sum(sc[:, LNC_S2:LNC_S2 + 1], junk[:, :], axis=Axis.X)
            v.drain().then_inc(jfree, 1)
            v.tensor_scalar_mul(sc[:, LNC_T1:LNC_T1 + 1], sc[:, LNC_S2:LNC_S2 + 1], 1.0 / DM)
            v.drain()
            v.tensor_tensor(sc[:, LNC_M2:LNC_M2 + 1], sc[:, LNC_NM:LNC_NM + 1],
                            sc[:, LNC_NM:LNC_NM + 1], Alu.mult)
            v.drain()
            v.tensor_tensor(sc[:, LNC_VAR:LNC_VAR + 1], sc[:, LNC_T1:LNC_T1 + 1],
                            sc[:, LNC_M2:LNC_M2 + 1], Alu.subtract)
            v.drain().then_inc(varsem, 1)

        @block.sync
        def _(sync):
            sync.dma_start(out=bq_sb[:, :], in_=bq_hbm.ap()).then_inc(p1_b, 16)
            sync.dma_start(out=bk_sb[:, :], in_=bk_hbm.ap()).then_inc(p1_b, 16)
            sync.dma_start(out=idb[:, :], in_=ident_dram.ap()).then_inc(p1_id, 16)
            sync.wait_ge(p1_w, 48)
            for wi, wt in ((0, wqT), (1, wkT), (2, wvT)):
                for ch in range(NCH):
                    sync.dma_start_transpose(
                        wt[:, 128 * ch:128 * (ch + 1)],
                        wstage[:, 1024 * wi + 128 * ch:1024 * wi + 128 * (ch + 1)],
                    ).then_inc(p1_wt, 16)
            for j, (mat, ci) in enumerate(pw_jobs):
                nblk = PW_NX[mat]
                sync.wait_ge(pw_l, 16 * (j + 1))
                if j >= 1:
                    sync.wait_ge(pw_s, 16 * j)
                for blk in range(nblk):
                    sync.dma_start_transpose(
                        pwout[:, 128 * blk:128 * (blk + 1)],
                        pwin[:, 128 * blk:128 * (blk + 1)],
                    ).then_inc(pw_x, 16)
            sync.wait_ge(vch, V_TOTAL)
            for tt in range(NT4):
                p = tt % 2
                if tt >= 2:
                    sync.wait_ge(p4_r1[p], tt // 2)
                sync.dma_start(out=xch_sb[p][:, :],
                               in_=xch_hbm.ap()[128 * tt:128 * (tt + 1), :]).then_inc(p4_xch[p], 16)
            def rtx_block(kk):
                base = 4096 * (kk % 2)
                if kk >= 2:
                    sync.wait_ge(p5_gu, 48 * (kk - 1))
                for t2 in range(4):
                    tg = 4 * kk + t2
                    pc = t2 % 2
                    sync.wait_ge(p5_cast[pc], tg // 2 + 1)
                    for ch in range(NCH):
                        sync.dma_start_transpose(
                            r1nT[:, base + P1T * ch + 128 * t2:base + P1T * ch + 128 * (t2 + 1)],
                            castst[pc][:, 128 * ch:128 * (ch + 1)],
                        ).then_inc(p5_rtx[pc], 16)

            rtx_block(0)
            for kk in range(NFB):
                if kk + 1 < NFB:
                    rtx_block(kk + 1)
                for t2 in range(4):
                    tg = 4 * kk + t2
                    p = t2 % 2
                    sync.wait_ge(p5_ln[p], tg // 2 + 1)
                    sync.dma_start(out=out_hbm.ap()[128 * tg:128 * (tg + 1), :],
                                   in_=out_sb[p][:, :]).then_inc(p5_out[p], 16)
            if debug:
                sync.wait_ge(dbg_sem, 80)
            sync.wait_ge(p5_out[0], 16 * (NT4 // 2))
            sync.wait_ge(p5_out[1], 16 * (NT4 // 2))

        @block.gpsimd
        def _(g):
            g.dma_start(out=wstage[:, 0:1024], in_=wq_hbm.ap()).then_inc(p1_w, 16)
            g.dma_start(out=wstage[:, 1024:2048], in_=wk_hbm.ap()).then_inc(p1_w, 16)
            g.dma_start(out=wstage[:, 2048:3072], in_=wv_hbm.ap()).then_inc(p1_w, 16)
            g.dma_start(out=bvb[:, :],
                        in_=bv_hbm.ap().unsqueeze(0).to_broadcast((128, 128))).then_inc(p1_bc, 16)
            g.dma_start(out=bob[:, :],
                        in_=bo_hbm.ap().unsqueeze(0).to_broadcast((128, 1024))).then_inc(p1_bc, 16)
            for tt in range(NP1):
                p = tt % 2
                if tt >= 2:
                    g.wait_ge(p1_tp[p], 4 * (tt // 2))
                for s4 in range(4):
                    g.dma_start(
                        out=xbt[p][:, 1024 * s4:1024 * (s4 + 1)],
                        in_=x_hbm.ap()[P1T * tt + 128 * s4:P1T * tt + 128 * (s4 + 1), :],
                    ).then_inc(p1_x[p], 16)
            def pw_load(j):
                mat, ci = pw_jobs[j]
                if j >= 1:
                    g.wait_ge(pw_x, 16 * _pwx_before[j])
                if mat == "wg":
                    g.dma_start(out=pwin[:, 0:1024],
                                in_=wg_hbm.ap()[128 * ci:128 * (ci + 1), :]).then_inc(pw_l, 16)
                elif mat == "wu":
                    g.dma_start(out=pwin[:, 0:1024],
                                in_=wu_hbm.ap()[128 * ci:128 * (ci + 1), :]).then_inc(pw_l, 16)
                elif mat == "wo":
                    g.dma_start(out=pwin[:, 0:1024],
                                in_=wo_hbm.ap()[128 * ci:128 * (ci + 1), :]).then_inc(pw_l, 16)
                else:
                    g.dma_start(out=pwin[:, :],
                                in_=wd_hbm.ap()[128 * ci:128 * (ci + 1), :]).then_inc(pw_l, 16)

            def pw_store(j):
                mat, ci = pw_jobs[j]
                nblk = PW_NX[mat]
                g.wait_ge(pw_x, 16 * (_pwx_before[j] + nblk))
                if mat == "wg":
                    dst = wgT_dram.ap()[:, :, 128 * ci:128 * (ci + 1)].transpose([1, 0, 2])
                    g.dma_start(out=dst, in_=pwout[:, 0:1024]
                                .rearrange("p (ch c) -> p ch c", ch=NCH, c=128)).then_inc(pw_s, 16)
                elif mat == "wu":
                    dst = wuT_dram.ap()[:, :, 128 * ci:128 * (ci + 1)].transpose([1, 0, 2])
                    g.dma_start(out=dst, in_=pwout[:, 0:1024]
                                .rearrange("p (ch c) -> p ch c", ch=NCH, c=128)).then_inc(pw_s, 16)
                elif mat == "wo":
                    dst = woT_dram.ap()[:, :, 128 * ci:128 * (ci + 1)].transpose([1, 0, 2])
                    g.dma_start(out=dst, in_=pwout[:, 0:1024]
                                .rearrange("p (ch c) -> p ch c", ch=NCH, c=128)).then_inc(pw_s, 16)
                else:
                    dst = wdT_dram.ap()[:, :, 128 * ci:128 * (ci + 1)].transpose([1, 0, 2])
                    g.dma_start(out=dst, in_=pwout[:, :]
                                .rearrange("p (ch c) -> p ch c", ch=NHC, c=128)).then_inc(pw_s, 16)

            for j in range(len(pw_jobs)):
                pw_load(j)
                pw_store(j)
            g.wait_ge(vch, VCP[("evac", 3)])
            g.dma_start(out=cc_in.ap()[0], in_=heads[:, 0:N // 2]).then_inc(p3_ccin, 16)
            g.wait_ge(p3_ccin, 16)
            g.collective_compute(
                "AllGather", Alu.bypass,
                replica_groups=[list(range(NCORES))],
                ins=[cc_in.ap()[0].opt()],
                outs=[cc_out.ap()[0].opt()],
            ).then_inc(p3_ag, 1)
            g.wait_ge(p3_ag, 1)
            pid = nc.partition_id([Pool])
            off = g.snap(pid * (S // 2))
            for r in range(NCORES):
                g.dma_start(out=heads[:, 1024 * r:1024 * (r + 1)],
                            in_=cc_out.ap()[0, r, :, bass.ds(off, S // 2)]).then_inc(p3_hm0, 16)
            g.wait_ge(vch, V_TOTAL)
            g.dma_start(out=cc_in.ap()[1], in_=heads[:, N // 2:]).then_inc(p3_ci1, 16)
            g.wait_ge(p3_ci1, 16)
            g.collective_compute(
                "AllGather", Alu.bypass,
                replica_groups=[list(range(NCORES))],
                ins=[cc_in.ap()[1].opt()],
                outs=[cc_out.ap()[1].opt()],
            ).then_inc(p3_ag, 1)
            g.wait_ge(vch, V_TOTAL)
            g.wait_ge(sch, S_TOTAL)
            g.wait_ge(pw_s, 16 * len(pw_jobs))
            for ch in range(NCH):
                g.dma_start(out=woT[:, 1024 * ch:1024 * (ch + 1)],
                            in_=woT_dram.ap()[ch]).then_inc(p3_wol, 16)
            g.wait_ge(p3_ag, 2)
            for r in range(NCORES):
                g.dma_start(out=heads[:, N // 2 + 1024 * r:N // 2 + 1024 * (r + 1)],
                            in_=cc_out.ap()[1, r, :, bass.ds(off, S // 2)]).then_inc(p3_hm1, 16)
            g.wait_ge(p4_mm[0], NT4 // 2)
            g.wait_ge(p4_mm[1], NT4 // 2)
            g.wait_ge(pw_s, 16 * len(pw_jobs))
            g.dma_start(out=wdT[:, :].rearrange("p (hc c) -> p hc c", hc=NHC, c=1024),
                        in_=wdT_dram.ap().transpose([1, 0, 2])).then_inc(p5_wdl, 16)
            for kk in range(NFB):
                for hc in range(NHC):
                    ph = hc % 2
                    idx = NHC * kk + hc
                    if idx >= 2:
                        g.wait_ge(p5_gu, 2 * idx - 2)
                    g.dma_start(out=wguT[ph][:, 0:1024]
                                .rearrange("p (ch c) -> p ch c", ch=NCH, c=128),
                                in_=wgT_dram.ap()[:, :, 128 * hc:128 * (hc + 1)]
                                .transpose([1, 0, 2])).then_inc(p5_wg[ph], 16)
                    g.dma_start(out=wguT[ph][:, 1024:2048]
                                .rearrange("p (ch c) -> p ch c", ch=NCH, c=128),
                                in_=wuT_dram.ap()[:, :, 128 * hc:128 * (hc + 1)]
                                .transpose([1, 0, 2])).then_inc(p5_wg[ph], 16)
            if debug:
                g.wait_ge(vch, V_TOTAL)
                g.dma_start(out=dbg["q"].ap(), in_=qc[:, :]).then_inc(dbg_sem, 16)
                g.dma_start(out=dbg["k"].ap(), in_=kc[:, :]).then_inc(dbg_sem, 16)
                g.dma_start(out=dbg["v"].ap(), in_=vc[:, :]).then_inc(dbg_sem, 16)
                g.dma_start(out=dbg["heads"].ap(), in_=heads[:, :]).then_inc(dbg_sem, 16)
                g.wait_ge(p4_fin, NT4)
                g.dma_start(out=dbg["r1n"].ap(), in_=r1n[:, :]).then_inc(dbg_sem, 16)

        @block.tensor
        def _(t):
            t.wait_ge(p1_wt, 16 * 24)
            t.wait_ge(p1_id, 16)
            for tt in range(NP1):
                p = tt % 2
                t.wait_ge(p1_x[p], 64 * (tt // 2 + 1))
                for grp in range(4):
                    gi = 4 * (tt // 2) + grp
                    if gi >= 2:
                        t.wait_ge(p1_te[p], gi - 1)
                    for ch in range(NCH):
                        mi = t.transpose(
                            tp_bank[grp % 2][:, 128 * ch:128 * (ch + 1)],
                            xbt[p][:, 1024 * grp + 128 * ch:1024 * grp + 128 * (ch + 1)],
                            idb[:, :])
                    mi.then_inc(p1_tp[p], 1)
                t.wait_ge(p1_te[p], 4 * (tt // 2) + 4)
                if tt >= 2:
                    t.wait_ge(p1_evq[p], tt // 2)
                    t.wait_ge(p1_evk[p], tt // 2)
                    t.wait_ge(p1_evv[p], 4 * (tt // 2))
                for ch in range(NCH):
                    mi = t.matmul(q_ps[p][:, :], wqT[:, 128 * ch:128 * (ch + 1)],
                                  xT[p][:, P1T * ch:P1T * (ch + 1)],
                                  start=(ch == 0), stop=(ch == NCH - 1))
                mi.then_inc(p1_mmq[p], 1)
                for ch in range(NCH):
                    mi = t.matmul(k_ps[p][:, :], wkT[:, 128 * ch:128 * (ch + 1)],
                                  xT[p][:, P1T * ch:P1T * (ch + 1)],
                                  start=(ch == 0), stop=(ch == NCH - 1))
                mi.then_inc(p1_mmk[p], 1)
                for s4 in range(4):
                    for ch in range(NCH):
                        mi = t.matmul(v_ps[p][:, 128 * s4:128 * (s4 + 1)],
                                      xT[p][:, P1T * ch + 128 * s4:P1T * ch + 128 * (s4 + 1)],
                                      wvT[:, 128 * ch:128 * (ch + 1)],
                                      start=(s4 == 0 and ch == 0), stop=(ch == NCH - 1),
                                      skip_group_check=True)
                    mi.then_inc(p1_mmv[p], 1)
            for p in range(2):
                t.wait_ge(p1_evq[p], NP1 // 2)
                t.wait_ge(p1_evk[p], NP1 // 2)
                t.wait_ge(p1_evv[p], 4 * (NP1 // 2))
            nA = 0
            nB = 0

            def scores(it):
                nonlocal nA, nB
                qb, kt = divmod(it, NKT)
                pp = it % 2
                for gidx in range(4):
                    b_lo = 2 * gidx
                    if gidx % 2 == 0:
                        psT, msem, n_use = scoresA, p2_mmA, nA
                        nA += 1
                    else:
                        psT, msem, n_use = scoresB, p2_mmB, nB
                        nB += 1
                    if n_use > 0:
                        itp = (n_use - 1) // 2
                        gp = ((n_use - 1) % 2) * 2 + (gidx % 2)
                        t.wait_ge(sch, SCP[("cp", itp, gp)])
                    for bi in range(2):
                        bb = b_lo + bi
                        for h in range(2):
                            mi = t.matmul(
                                psT[:, h * 512 + bi * 256:h * 512 + bi * 256 + 256],
                                kc[64 * h:64 * (h + 1), S * bb + KT * kt:S * bb + KT * (kt + 1)],
                                qc[64 * h:64 * (h + 1), S * bb + QB * qb:S * bb + QB * (qb + 1)],
                                start=True, stop=True, tile_position=(64 * h, 0),
                                skip_group_check=True,
                            )
                    mi.then_inc(msem, 1)

            scores(0)
            scores(1)
            for it in range(NIT):
                qb, kt = divmod(it, NKT)
                pp = it % 2
                if it + 2 < NIT:
                    scores(it + 2)
                if kt == 0 and qb > 0:
                    t.wait_ge(vch, VCP[("evac", qb - 1)])
                t.wait_ge(vch, VCP[("mul", it)])
                for bb in range(B):
                    gidx, bi = divmod(bb, 2)
                    vt = 16 * bb + kt
                    for h in range(2):
                        mi = t.matmul(
                            heads_ps[64 * h:64 * (h + 1), 256 * bb:256 * (bb + 1)],
                            vc[:, 128 * vt + 64 * h:128 * vt + 64 * (h + 1)],
                            e2_sb[:, pp * 4096 + 2048 * h + 512 * gidx + 256 * bi:
                                  pp * 4096 + 2048 * h + 512 * gidx + 256 * bi + 256],
                            start=(kt == 0 and bi == 0),
                            stop=(kt == NKT - 1),
                            tile_position=(0, 64 * h),
                            skip_group_check=True,
                        )
                mi.then_inc(p2_av, 1)
            t.wait_ge(p3_hm0, 16 * NCORES)
            t.wait_ge(p3_wol, 16 * NCH)
            t.wait_ge(sch, S_TOTAL)
            for tt in range(NT4):
                p = tt % 2
                if tt == 8:
                    t.wait_ge(p3_hm1, 16 * NCORES)
                if tt >= 2:
                    t.wait_ge(p4_r1[p], tt // 2)
                for r in range(NCORES):
                    for half in range(2):
                        mi = t.matmul(
                            wo_ps[p][:, 512 * half:512 * (half + 1)],
                            heads[:, 8192 * (tt // 8) + 1024 * r + 128 * (tt % 8):
                                  8192 * (tt // 8) + 1024 * r + 128 * (tt % 8) + 128],
                            woT[:, 1024 * r + 512 * half:1024 * r + 512 * (half + 1)],
                            start=(r == 0), stop=(r == NCORES - 1),
                            skip_group_check=True,
                        )
                mi.then_inc(p4_mm[p], 1)
            t.wait_ge(p5_wdl, 16)
            t.wait_ge(p4_r1[0], NT4 // 2)
            t.wait_ge(p4_r1[1], NT4 // 2)

            def gu_block(kk):
                t.wait_ge(p5_rtx[0], 256 * (kk + 1))
                t.wait_ge(p5_rtx[1], 256 * (kk + 1))
                for hc in range(NHC):
                    ph = hc % 2
                    idx = NHC * kk + hc
                    t.wait_ge(p5_wg[ph], 32 * (idx // 2 + 1))
                    if idx >= 2:
                        t.wait_ge(p5_silu[ph], idx // 2)
                        t.wait_ge(p5_upev[ph], idx // 2)
                    base = 4096 * (kk % 2)
                    for ch in range(NCH):
                        mi = t.matmul(gate_ps[ph][:, :], wguT[ph][:, 128 * ch:128 * (ch + 1)],
                                      r1nT[:, base + P1T * ch:base + P1T * (ch + 1)],
                                      start=(ch == 0), stop=(ch == NCH - 1))
                    mi.then_inc(p5_gu, 1)
                    for ch in range(NCH):
                        mi = t.matmul(up_ps[ph][:, :],
                                      wguT[ph][:, 1024 + 128 * ch:1024 + 128 * (ch + 1)],
                                      r1nT[:, base + P1T * ch:base + P1T * (ch + 1)],
                                      start=(ch == 0), stop=(ch == NCH - 1))
                    mi.then_inc(p5_gu, 1)

            def down_block(kk):
                t.wait_ge(p5_h, NHC * (kk + 1))
                for t2 in range(4):
                    tg = 4 * kk + t2
                    p = t2 % 2
                    if tg >= 2:
                        t.wait_ge(p5_dnev[p], tg // 2)
                    for hc in range(NHC):
                        for half in range(2):
                            mi = t.matmul(
                                down_ps[p][:, 512 * half:512 * (half + 1)],
                                hfull[:, FBLK * hc + 128 * t2:FBLK * hc + 128 * (t2 + 1)],
                                wdT[:, 1024 * hc + 512 * half:1024 * hc + 512 * (half + 1)],
                                start=(hc == 0), stop=(hc == NHC - 1),
                                skip_group_check=True,
                            )
                    mi.then_inc(p5_dn[p], 1)

            for kk in range(NFB):
                gu_block(kk)
                down_block(kk)

        @block.scalar
        def _(s):
            s.wait_ge(p1_b, 32)
            for te in range(NP1):
                p = te % 2
                s.wait_ge(p1_mmq[p], te // 2 + 1)
                s.activation(qc[:, P1T * te:P1T * (te + 1)], q_ps[p][:, :],
                             Act.Identity, bias=bq_sb[:, :], scale=1.0).then_inc(p1_evq[p], 1)
                s.wait_ge(p1_mmk[p], te // 2 + 1)
                s.activation(kc[:, P1T * te:P1T * (te + 1)], k_ps[p][:, :],
                             Act.Identity, bias=bk_sb[:, :], scale=1.0).then_inc(p1_evk[p], 1)
            nA = 0
            nB = 0

            def s_copies(it):
                nonlocal nA, nB
                pp = it % 2
                if it >= 2:
                    s.wait_ge(vch, VCP[("sub", it - 2)])
                for gidx in range(4):
                    if gidx % 2 == 0:
                        psT, msem, n_use = scoresA, p2_mmA, nA
                        nA += 1
                    else:
                        psT, msem, n_use = scoresB, p2_mmB, nB
                        nB += 1
                    s.wait_ge(msem, n_use + 1)
                    s.copy(sc_sb[:, :].rearrange("p (a h g c) -> p a h g c",
                                                 a=2, h=2, g=4, c=512)[:, pp, :, gidx],
                           psT[:, :].rearrange("p (h c) -> p h c", h=2, c=512)).then_inc(sch, 1)

            s_copies(0)
            s_copies(1)
            for it in range(NIT):
                pp = it % 2
                if it >= 2:
                    s.wait_ge(p2_av, it - 1)
                s.wait_ge(vch, VCP[("sub", it)])
                s.activation(e2_sb[:, pp * 4096:pp * 4096 + 4096],
                             e_sb[:, pp * 4096:pp * 4096 + 4096], Act.Exp).then_inc(sch, 1)
                if it + 2 < NIT:
                    s_copies(it + 2)
                s.wait_ge(vch, VCP[("zl3", it)])
                if it >= 2:
                    s.wait_ge(vch, VCP[("mul", it - 2)])
                s.activation(zbuf[:, 512 * pp:512 * (pp + 1)],
                             zbuf[:, 512 * pp:512 * (pp + 1)], Act.Ln).then_inc(sch, 1)
                s.wait_ge(sch, SCP[("ln", it)])
                s.activation(rbuf[:, 512 * pp:512 * (pp + 1)],
                             zbuf[:, 512 * pp:512 * (pp + 1)], Act.Exp,
                             scale=-1.0).then_inc(sch, 1)
            s.wait_ge(misc, 1)
            for tt in range(NT4):
                p = tt % 2
                s.wait_ge(p4_r1[p], tt // 2 + 1)
                s.wait_ge(p4_jfree, tt)
                s.activation(junk[:, :], r1st[p][:, :], Act.Square).then_inc(p4_sq, 1)
                s.wait_ge(p4_var[p], tt // 2 + 1)
                if tt >= 2:
                    s.wait_ge(p4_fin, tt - 1)
                s.activation(lnsc[p][:, LNC_SD:LNC_SD + 1], lnsc[p][:, LNC_VAR:LNC_VAR + 1],
                             Act.Ln, bias=epsv[:, :], scale=1.0)
                s.drain()
                s.activation(lnsc[p][:, LNC_SD:LNC_SD + 1], lnsc[p][:, LNC_SD:LNC_SD + 1],
                             Act.Exp, scale=-0.5)
                s.drain().then_inc(p4_sd[p], 1)
            def silu_block(kk):
                for hc in range(NHC):
                    ph = hc % 2
                    idx = NHC * kk + hc
                    s.wait_ge(p5_gu, 2 * idx + 1)
                    if idx >= 2:
                        s.wait_ge(p5_h, idx - 1)
                    s.activation(silu_st[ph][:, :], gate_ps[ph][:, :], Act.Silu,
                                 scale=SC).then_inc(p5_silu[ph], 1)

            for kk in range(NFB):
                silu_block(kk)
                for t2 in range(4):
                    tg = 4 * kk + t2
                    p = t2 % 2
                    s.wait_ge(p5_dnev[p], tg // 2 + 1)
                    s.wait_ge(p5_jfree, tg)
                    s.activation(junk[:, :], r1st[p][:, :], Act.Square).then_inc(p5_sq, 1)
                    s.wait_ge(p5_var[p], tg // 2 + 1)
                    if tg >= 2:
                        s.wait_ge(p5_ln[p], tg // 2)
                    s.activation(lnsc[p][:, LNC_SD:LNC_SD + 1], lnsc[p][:, LNC_VAR:LNC_VAR + 1],
                                 Act.Ln, bias=epsv[:, :], scale=1.0)
                    s.drain()
                    s.activation(lnsc[p][:, LNC_SD:LNC_SD + 1], lnsc[p][:, LNC_SD:LNC_SD + 1],
                                 Act.Exp, scale=-0.5)
                    s.drain().then_inc(p5_sd[p], 1)

        @block.vector
        def _(v):
            v.memset(epsv[:, :], EPS)
            v.drain().then_inc(misc, 1)
            v.wait_ge(p1_bc, 32)
            for tt in range(NP1 + 1):
                if tt < NP1:
                    p = tt % 2
                    for grp in range(4):
                        gi = 4 * (tt // 2) + grp
                        v.wait_ge(p1_tp[p], gi + 1)
                        v.tensor_copy(
                            xT[p][:, :].rearrange("p (ch sp c) -> p ch sp c",
                                                  ch=8, sp=4, c=128)[:, :, grp],
                            tp_bank[grp % 2][:, :].rearrange("p (ch c) -> p ch c",
                                                             ch=8, c=128))
                        v.drain().then_inc(p1_te[p], 1)
                if tt >= 1:
                    te = tt - 1
                    p = te % 2
                    v.wait_ge(p1_mmv[p], 4 * (te // 2) + 4)
                    v.tensor_tensor(
                        vc[:, 512 * te:512 * (te + 1)].rearrange("p (s c) -> p s c", s=4),
                        v_ps[p][:, :].rearrange("p (s c) -> p s c", s=4),
                        bvb[:, :].unsqueeze(1).to_broadcast((128, 4, 128)),
                        Alu.add)
                    v.drain().then_inc(p1_evv[p], 4)
            vn = 0

            def vop(inst):
                nonlocal vn
                inst.then_inc(vch, 1)
                vn += 1

            mt_l1 = mtmp[:, 0:2048].rearrange("p (h c) -> p h c", h=2, c=1024)
            mt_l2 = mtmp[:, 2048:3072].rearrange("p (h c) -> p h c", h=2, c=512)
            mt_m = mtmp[:, 3072:3584].rearrange("p (h c) -> p h c", h=2, c=256)

            def v_maxsub(j):
                pp = j % 2
                v.wait_ge(sch, SCP[("cp", j, 3)])
                if j >= 2:
                    v.wait_ge(sch, SCP[("exp", j - 2)])
                sc_h = _slab_h(sc_sb, pp)
                v.wait_ge(vch, vn)
                vop(v.tensor_tensor(mt_l1, sc_h[:, :, 0:1024], sc_h[:, :, 1024:2048], Alu.max))
                v.wait_ge(vch, vn)
                vop(v.tensor_tensor(mt_l2, mt_l1[:, :, 0:512], mt_l1[:, :, 512:1024], Alu.max))
                v.wait_ge(vch, vn)
                vop(v.tensor_tensor(mt_m, mt_l2[:, :, 0:256], mt_l2[:, :, 256:512], Alu.max))
                v.wait_ge(vch, vn)
                vop(v.tensor_tensor(
                    _slab_hgbc(e_sb, pp), _slab_hgbc(sc_sb, pp),
                    mt_m.unsqueeze(2).to_broadcast((128, 2, 8, QB)),
                    Alu.subtract))

            v_maxsub(0)
            for it in range(NIT):
                qb, kt = divmod(it, NKT)
                pp = it % 2
                if it + 1 < NIT:
                    v_maxsub(it + 1)
                e2_h = _slab_h(e2_sb, pp)
                v.wait_ge(sch, SCP[("exp", it)])
                v.wait_ge(vch, vn)
                vop(v.tensor_tensor(mt_l1, e2_h[:, :, 0:1024], e2_h[:, :, 1024:2048], Alu.add))
                v.wait_ge(vch, vn)
                vop(v.tensor_tensor(mt_l2, mt_l1[:, :, 0:512], mt_l1[:, :, 512:1024], Alu.add))
                v.wait_ge(vch, vn)
                vop(v.tensor_tensor(
                    zbuf[:, 512 * pp:512 * (pp + 1)].rearrange("p (h c) -> p h c", h=2, c=QB),
                    mt_l2[:, :, 0:256], mt_l2[:, :, 256:512], Alu.add))
                v.wait_ge(sch, SCP[("rexp", it)])
                v.wait_ge(vch, vn)
                vop(v.tensor_tensor(
                    _slab_hgbc(e2_sb, pp), _slab_hgbc(e2_sb, pp),
                    rbuf[:, 512 * pp:512 * (pp + 1)].rearrange("p (h c) -> p h c", h=2, c=QB)
                    .unsqueeze(2).to_broadcast((128, 2, 8, QB)),
                    Alu.mult))
                if kt == NKT - 1:
                    v.wait_ge(p2_av, it + 1)
                    v.wait_ge(vch, vn)
                    vop(v.tensor_copy(
                        heads[:, :].rearrange("p (a b q) -> p a b q", a=2, b=B, q=S // 2)
                        [:, qb // 4, :, QB * (qb % 4):QB * (qb % 4 + 1)],
                        heads_ps[:, :].rearrange("p (b q) -> p b q", b=B)))
            assert vn == V_TOTAL, (vn, V_TOTAL)
            for tt in range(NT4):
                p = tt % 2
                v.wait_ge(p4_xch[p], 16 * (tt // 2 + 1))
                v.wait_ge(p4_mm[p], tt // 2 + 1)
                v.tensor_tensor(r1st[p][:, :], xch_sb[p][:, :], bob[:, :], Alu.add)
                v.drain()
                v.scalar_tensor_tensor(r1st[p][:, :], wo_ps[p][:, :], 1.0, r1st[p][:, :],
                                       Alu.mult, Alu.add)
                v.drain().then_inc(p4_r1[p], 1)
                ln_stats(v, p, r1st[p], p4_var[p], p4_sq, tt + 1, p4_jfree)
                v.wait_ge(p4_sd[p], tt // 2 + 1)
                v.tensor_scalar(r1n[:, 1024 * tt:1024 * (tt + 1)], r1st[p][:, :],
                                lnsc[p][:, LNC_NM:LNC_NM + 1],
                                lnsc[p][:, LNC_SD:LNC_SD + 1], Alu.add, Alu.mult)
                v.drain().then_inc(p4_fin, 1)
            def r1n_cast1(tg):
                pc = tg % 2
                if tg >= 2:
                    v.wait_ge(p5_rtx[pc], 128 * (tg // 2))
                v.tensor_copy(castst[pc][:, :], r1n[:, 1024 * tg:1024 * (tg + 1)])
                v.drain().then_inc(p5_cast[pc], 1)

            for t2 in range(4):
                r1n_cast1(t2)

            def hf_block(kk):
                if kk > 0:
                    v.wait_ge(p5_dn[0], 2 * kk)
                    v.wait_ge(p5_dn[1], 2 * kk)
                for hc in range(NHC):
                    ph = hc % 2
                    idx = NHC * kk + hc
                    if kk + 1 < NFB and hc in (0, 6, 12, 18):
                        r1n_cast1(4 * (kk + 1) + hc // 6)
                    v.wait_ge(p5_gu, 2 * idx + 2)
                    v.tensor_copy(upbf[ph][:, :], up_ps[ph][:, :])
                    v.drain().then_inc(p5_upev[ph], 1)
                    v.wait_ge(p5_silu[ph], idx // 2 + 1)
                    v.tensor_tensor(hfull[:, FBLK * hc:FBLK * (hc + 1)], silu_st[ph][:, :],
                                    upbf[ph][:, :], Alu.mult)
                    v.drain().then_inc(p5_h, 1)

            for kk in range(NFB):
                hf_block(kk)
                for t2 in range(4):
                    tg = 4 * kk + t2
                    p = t2 % 2
                    v.wait_ge(p5_dn[p], tg // 2 + 1)
                    v.scalar_tensor_tensor(r1st[p][:, :], down_ps[p][:, :], SC * SC,
                                           r1n[:, 1024 * tg:1024 * (tg + 1)], Alu.mult, Alu.add)
                    v.drain().then_inc(p5_dnev[p], 1)
                    ln_stats(v, p, r1st[p], p5_var[p], p5_sq, tg + 1, p5_jfree)
                    v.wait_ge(p5_sd[p], tg // 2 + 1)
                    if tg >= 2:
                        v.wait_ge(p5_out[p], 16 * (tg // 2))
                    v.tensor_scalar(out_sb[p][:, :], r1st[p][:, :],
                                    lnsc[p][:, LNC_NM:LNC_NM + 1],
                                    lnsc[p][:, LNC_SD:LNC_SD + 1], Alu.add, Alu.mult)
                    v.drain().then_inc(p5_ln[p], 1)

    return nc


_cache = {}


def kernel(**inputs):
    x = np.ascontiguousarray(np.asarray(inputs["L0_attn_x"], dtype=np.float32))
    xf = x.reshape(N, DM)
    Wq = np.ascontiguousarray(np.asarray(inputs["L0_attn_Wq"], dtype=np.float32))
    Wk = np.ascontiguousarray(np.asarray(inputs["L0_attn_Wk"], dtype=np.float32))
    Wv = np.ascontiguousarray(np.asarray(inputs["L0_attn_Wv"], dtype=np.float32))
    Wo = np.ascontiguousarray(np.asarray(inputs["L0_attn_Wo"], dtype=np.float32))
    bq = np.asarray(inputs["L0_attn_bq"], dtype=np.float32)
    bk = np.asarray(inputs["L0_attn_bk"], dtype=np.float32)
    bv = np.asarray(inputs["L0_attn_bv"], dtype=np.float32)
    bo = np.ascontiguousarray(np.asarray(inputs["L0_attn_bo"], dtype=np.float32))
    wg = np.ascontiguousarray(np.asarray(inputs["L0_ffn_w_gate"], dtype=np.int32))
    wu = np.ascontiguousarray(np.asarray(inputs["L0_ffn_w_up"], dtype=np.int32))
    wd = np.ascontiguousarray(np.asarray(inputs["L0_ffn_w_down"], dtype=np.int32))

    if "nc" not in _cache:
        _cache["nc"] = build_nc()
    nc = _cache["nc"]

    in_maps = []
    for c in range(NCORES):
        J = slice(128 * c, 128 * (c + 1))
        in_maps.append({
            "xfull": xf,
            "xch": np.ascontiguousarray(xf[CH * c:CH * (c + 1)]),
            "wq_s": np.ascontiguousarray(Wq[J]),
            "wk_s": np.ascontiguousarray(Wk[J]),
            "wv_s": np.ascontiguousarray(Wv[J]),
            "bq_s": np.ascontiguousarray(bq[J].reshape(128, 1)),
            "bk_s": np.ascontiguousarray(bk[J].reshape(128, 1)),
            "bv_s": np.ascontiguousarray(bv[J]),
            "wo": Wo,
            "bo": bo,
            "wgate": wg,
            "wup": wu,
            "wdown": wd,
        })
    res = run_bass_kernel_spmd(nc, in_maps, core_ids=list(range(NCORES)))
    out = np.stack([res.results[c]["out"] for c in range(NCORES)], axis=0)
    return out.reshape(B, S, DM)


# version 24
# speedup vs baseline: 1.0099x; 1.0099x over previous
import numpy as np
from contextlib import ExitStack

import concourse.bass as bass
import concourse.mybir as mybir
from concourse.bass_utils import run_bass_kernel_spmd

F32 = mybir.dt.float32
BF16 = mybir.dt.bfloat16
I32 = mybir.dt.int32
Alu = mybir.AluOpType
Act = mybir.ActivationFunctionType
Axis = mybir.AxisListType
Pool = mybir.EngineType.Pool

NCORES = 8
DM = 1024
NH = 16
DK = 64
B = 8
S = 2048
N = B * S
HID = 3072
NCH = DM // 128
SC = float(np.float32(1.0) / np.float32(127.0))

P1T = 512
NP1 = N // P1T
QB = 256
NQB = S // QB
KT = 128
NKT = S // KT
NIT = NQB * NKT
CH = S
NT4 = CH // 128
FBLK = 512
NFB = CH // FBLK
NHC = HID // 128
EPS = 1e-5

DEBUG = False


def _slab(t, pp):
    return t[:, pp * 4096:(pp + 1) * 4096]


def _slab_h(t, pp):
    return t[:, :].rearrange("p (a h c) -> p a h c", a=2, h=2, c=2048)[:, pp]


def _slab_hgbc(t, pp):
    return t[:, :].rearrange("p (a h gb c) -> p a h gb c", a=2, h=2, gb=8, c=256)[:, pp]


def _v_sched():
    ops = []

    def maxsub(j):
        return [("max1", j), ("max2", j), ("max3", j), ("sub", j)]

    ops += maxsub(0)
    for it in range(NIT):
        if it + 1 < NIT:
            ops += maxsub(it + 1)
        ops += [("sum1", it), ("sum2", it), ("zl3", it), ("mul", it)]
        if it % NKT == NKT - 1:
            ops.append(("evac", it // NKT))
    return {op: i + 1 for i, op in enumerate(ops)}, len(ops)


def _s_sched():
    ops = []
    for j in (0, 1):
        for g in range(4):
            ops.append(("cp", j, g))
    for it in range(NIT):
        ops.append(("exp", it))
        if it + 2 < NIT:
            for g in range(4):
                ops.append(("cp", it + 2, g))
        ops.append(("ln", it))
        ops.append(("rexp", it))
    return {op: i + 1 for i, op in enumerate(ops)}, len(ops)


VCP, V_TOTAL = _v_sched()
SCP, S_TOTAL = _s_sched()


def build_nc(debug=None):
    if debug is None:
        debug = DEBUG
    nc = bass.Bass("TRN2", target_bir_lowering=False, debug=False)

    x_hbm = nc.dram_tensor("xfull", [N, DM], F32, kind="ExternalInput")
    xch_hbm = nc.dram_tensor("xch", [CH, DM], F32, kind="ExternalInput")
    wq_hbm = nc.dram_tensor("wq_s", [128, DM], F32, kind="ExternalInput")
    wk_hbm = nc.dram_tensor("wk_s", [128, DM], F32, kind="ExternalInput")
    wv_hbm = nc.dram_tensor("wv_s", [128, DM], F32, kind="ExternalInput")
    bq_hbm = nc.dram_tensor("bq_s", [128, 1], F32, kind="ExternalInput")
    bk_hbm = nc.dram_tensor("bk_s", [128, 1], F32, kind="ExternalInput")
    bv_hbm = nc.dram_tensor("bv_s", [128], F32, kind="ExternalInput")
    wo_hbm = nc.dram_tensor("wo", [DM, DM], F32, kind="ExternalInput")
    bo_hbm = nc.dram_tensor("bo", [DM], F32, kind="ExternalInput")
    wg_hbm = nc.dram_tensor("wgate", [HID, DM], I32, kind="ExternalInput")
    wu_hbm = nc.dram_tensor("wup", [HID, DM], I32, kind="ExternalInput")
    wd_hbm = nc.dram_tensor("wdown", [DM, HID], I32, kind="ExternalInput")
    out_hbm = nc.dram_tensor("out", [CH, DM], F32, kind="ExternalOutput")

    import ml_dtypes
    ident_dram = nc.inline_tensor(np.eye(128, dtype=ml_dtypes.bfloat16), name="ident")
    cc_in = nc.dram_tensor("cc_in", [2, 128, N // 2], BF16)
    wgT_dram = nc.dram_tensor("wgT_dram", [NCH, 128, HID], BF16)
    wuT_dram = nc.dram_tensor("wuT_dram", [NCH, 128, HID], BF16)
    wdT_dram = nc.dram_tensor("wdT_dram", [NHC, 128, DM], BF16)
    woT_dram = nc.dram_tensor("woT_dram", [NCH, 128, DM], BF16)
    cc_out = nc.dram_tensor("cc_out", [2, NCORES, 128, N // 2], BF16, addr_space="Shared")

    dbg = {}
    if debug:
        dbg["q"] = nc.dram_tensor("dbg_q", [128, N], F32, kind="ExternalOutput")
        dbg["k"] = nc.dram_tensor("dbg_k", [128, N], F32, kind="ExternalOutput")
        dbg["v"] = nc.dram_tensor("dbg_v", [128, N], F32, kind="ExternalOutput")
        dbg["heads"] = nc.dram_tensor("dbg_heads", [128, N], F32, kind="ExternalOutput")
        dbg["r1n"] = nc.dram_tensor("dbg_r1n", [128, NT4 * 1024], F32, kind="ExternalOutput")

    ctx = ExitStack()
    with ctx:
        E = ctx.enter_context

        bq_sb = E(nc.sbuf_tensor("bq_sb", [128, 1], F32))
        bk_sb = E(nc.sbuf_tensor("bk_sb", [128, 1], F32))
        bvb = E(nc.sbuf_tensor("bvb", [128, 128], F32))
        bob = E(nc.sbuf_tensor("bob", [128, 1024], F32))
        epsv = E(nc.sbuf_tensor("epsv", [128, 1], F32))
        idb = E(nc.sbuf_tensor("idb", [128, 128], BF16))

        def sem(name):
            return E(nc.semaphore(name))

        p1_x = [sem("p1_x0"), sem("p1_x1")]
        p1_w = sem("p1_w")
        p1_bc = sem("p1_bc")
        p1_b = sem("p1_b")
        misc = sem("misc")
        p1_id = sem("p1_id")
        p1_tp = [sem("p1_tp0"), sem("p1_tp1")]
        p1_te = [sem("p1_te0"), sem("p1_te1")]
        p1_wt = sem("p1_wt")
        p1_mmq = [sem("p1_mmq0"), sem("p1_mmq1")]
        p1_mmk = [sem("p1_mmk0"), sem("p1_mmk1")]
        p1_mmv = [sem("p1_mmv0"), sem("p1_mmv1")]
        p1_pe = [sem("p1_pe0"), sem("p1_pe1")]
        p1_evq = [sem("p1_evq0"), sem("p1_evq1")]
        p1_evk = [sem("p1_evk0"), sem("p1_evk1")]
        p1_evv = [sem("p1_evv0"), sem("p1_evv1")]
        p2_mmA = sem("p2_mmA")
        p2_mmB = sem("p2_mmB")
        vch = sem("vch")
        gch = sem("gch")
        sch = sem("sch")
        p2_av = sem("p2_av")
        p3_ccin = sem("p3_ccin")
        p3_ci1 = sem("p3_ci1")
        p3_ag = sem("p3_ag")
        p3_hm0 = sem("p3_hm0")
        p3_hm1 = sem("p3_hm1")
        p3_wol = sem("p3_wol")
        p4_xch = [sem("p4_xch0"), sem("p4_xch1")]
        p4_mm = [sem("p4_mm0"), sem("p4_mm1")]
        p4_r1 = [sem("p4_r10"), sem("p4_r11")]
        p4_var = [sem("p4_var0"), sem("p4_var1")]
        p4_sd = [sem("p4_sd0"), sem("p4_sd1")]
        p4_fin = sem("p4_fin")
        p4_sq = sem("p4_sq")
        p4_jfree = sem("p4_jfree")
        p5_sq = sem("p5_sq")
        p5_jfree = sem("p5_jfree")
        p5_cast = [sem("p5_cast0"), sem("p5_cast1")]
        p5_rtx = [sem("p5_rtx0"), sem("p5_rtx1")]
        p5_wg = [sem("p5_wg0"), sem("p5_wg1")]
        pw_l = sem("pw_l")
        pw_x = sem("pw_x")
        pw_s = sem("pw_s")
        p5_gu = sem("p5_gu")
        p5_silu = [sem("p5_silu0"), sem("p5_silu1")]
        p5_upev = [sem("p5_upev0"), sem("p5_upev1")]
        p5_h = sem("p5_h")
        p5_wdl = sem("p5_wdl")
        p5_wdx = sem("p5_wdx")
        p5_dn = [sem("p5_dn0"), sem("p5_dn1")]
        p5_dnev = [sem("p5_dnev0"), sem("p5_dnev1")]
        p5_var = [sem("p5_var0"), sem("p5_var1")]
        p5_sd = [sem("p5_sd0"), sem("p5_sd1")]
        p5_ln = [sem("p5_ln0"), sem("p5_ln1")]
        p5_out = [sem("p5_out0"), sem("p5_out1")]
        dbg_sem = sem("dbg") if debug else None

        ctxA = E(ExitStack())
        qc = ctxA.enter_context(nc.sbuf_tensor("qc", [128, N], BF16))
        kc = ctxA.enter_context(nc.sbuf_tensor("kc", [128, N], BF16))
        vc = ctxA.enter_context(nc.sbuf_tensor("vc", [128, N], BF16))
        heads = ctxA.enter_context(nc.sbuf_tensor("heads", [128, N], BF16))
        ctxB1 = ExitStack()
        wstage = ctxB1.enter_context(nc.sbuf_tensor("wstage", [128, 3 * 1024], BF16))
        wqT = ctxB1.enter_context(nc.sbuf_tensor("wqT", [128, 1024], BF16))
        wkT = ctxB1.enter_context(nc.sbuf_tensor("wkT", [128, 1024], BF16))
        wvT = ctxB1.enter_context(nc.sbuf_tensor("wvT", [128, 1024], BF16))
        xbt = [ctxB1.enter_context(nc.sbuf_tensor(f"xbt{p}", [128, 4096], BF16)) for p in range(2)]
        xT = [ctxB1.enter_context(nc.sbuf_tensor(f"xT{p}", [128, 4096], BF16)) for p in range(2)]
        ctxB1.close()
        sc_sb = ctxA.enter_context(nc.sbuf_tensor("sc_sb", [128, 8192], BF16))
        e_sb = ctxA.enter_context(nc.sbuf_tensor("e_sb", [128, 8192], BF16))
        e2_sb = ctxA.enter_context(nc.sbuf_tensor("e2_sb", [128, 8192], BF16))
        mtmp = ctxA.enter_context(nc.sbuf_tensor("mtmp", [128, 4096], BF16))
        zbuf = ctxA.enter_context(nc.sbuf_tensor("zbuf", [128, 1024], F32))
        pwin = ctxA.enter_context(nc.sbuf_tensor("pwin", [128, 3072], BF16))
        pwout = ctxA.enter_context(nc.sbuf_tensor("pwout", [128, 3072], BF16))
        rbuf = ctxA.enter_context(nc.sbuf_tensor("rbuf", [128, 1024], BF16))
        psA = E(ExitStack())
        q_ps = [psA.enter_context(nc.psum_tensor(f"q_ps{p}", [128, 512], F32)) for p in range(2)]
        k_ps = [psA.enter_context(nc.psum_tensor(f"k_ps{p}", [128, 512], F32)) for p in range(2)]
        v_ps = [psA.enter_context(nc.psum_tensor(f"v_ps{p}", [128, 512], F32)) for p in range(2)]

        psA_close = psA


        ctx.callback(lambda: None)

        psA_close.close()
        psC = E(ExitStack())
        scoresA = psC.enter_context(nc.psum_tensor("scoresA", [128, 1024], F32))
        scoresB = psC.enter_context(nc.psum_tensor("scoresB", [128, 1024], F32))
        heads_ps = psC.enter_context(nc.psum_tensor("heads_ps", [128, 2048], F32))
        q_ps = [scoresA[:, 0:512], scoresA[:, 512:1024]]
        k_ps = [scoresB[:, 0:512], scoresB[:, 512:1024]]
        v_ps = [heads_ps[:, 0:512], heads_ps[:, 512:1024]]

        ctxA.close()
        r1n = E(nc.sbuf_tensor("r1n", [128, NT4 * 1024], F32))
        xch_sb = [E(nc.sbuf_tensor(f"xch_sb{p}", [128, 1024], F32)) for p in range(2)]
        r1st = [E(nc.sbuf_tensor(f"r1st{p}", [128, 1024], F32)) for p in range(2)]
        junk = E(nc.sbuf_tensor("junk", [128, 1024], F32))
        lnsc = [E(nc.sbuf_tensor(f"lnsc{p}", [128, 8], F32)) for p in range(2)]
        castst = [E(nc.sbuf_tensor(f"castst{p}", [128, 1024], BF16)) for p in range(2)]
        out_sb = [E(nc.sbuf_tensor(f"out_sb{p}", [128, 1024], F32)) for p in range(2)]
        ctxE1 = E(ExitStack())
        heads_my = ctxE1.enter_context(nc.sbuf_tensor("heads_my", [128, N], BF16))
        del heads_my
        woT = ctxE1.enter_context(nc.sbuf_tensor("woT", [128, 8192], BF16))
        ctxE1.close()
        wdT = E(nc.sbuf_tensor("wdT", [128, NHC * 1024], BF16))
        r1nT = E(nc.sbuf_tensor("r1nT", [128, 8192], BF16))
        hfull = E(nc.sbuf_tensor("hfull", [128, NHC * FBLK], BF16))
        wguT = [E(nc.sbuf_tensor(f"wguT{p}", [128, 2048], BF16)) for p in range(2)]
        silu_st = [E(nc.sbuf_tensor(f"silu_st{p}", [128, FBLK], BF16)) for p in range(2)]
        upbf = [E(nc.sbuf_tensor(f"upbf{p}", [128, FBLK], BF16)) for p in range(2)]
        wo_ps = [scoresA, scoresB]
        tp_bank = [heads_ps[:, 1024:1536].bitcast(BF16),
                   heads_ps[:, 1536:2048].bitcast(BF16)]
        gate_ps = [heads_ps[:, 0:512], heads_ps[:, 512:1024]]
        up_ps = [heads_ps[:, 1024:1536], heads_ps[:, 1536:2048]]
        down_ps = [scoresA, scoresB]

        pw_jobs = []
        for hc in range(NHC):
            pw_jobs.append(("wg", hc))
        for hc in range(NHC):
            pw_jobs.append(("wu", hc))
        for dch in range(NCH):
            pw_jobs.append(("wd", dch))
        for dch in range(NCH):
            pw_jobs.append(("wo", dch))
        PW_NX = {"wg": NCH, "wu": NCH, "wd": NHC, "wo": NCH}
        pw_xtot = [0]
        _pwx_before = {}
        for j, job in enumerate(pw_jobs):
            _pwx_before[j] = pw_xtot[0]
            pw_xtot[0] += PW_NX[job[0]]

        block = E(nc.Block())

        LNC_NM, LNC_S2, LNC_T1, LNC_M2, LNC_VAR, LNC_SD = range(6)

        def ln_stats(v, p, src, varsem, sqsem, sq_val, jfree):
            sc = lnsc[p]
            v.reduce_sum(sc[:, LNC_T1:LNC_T1 + 1], src[:, :], axis=Axis.X)
            v.drain()
            v.tensor_scalar_mul(sc[:, LNC_NM:LNC_NM + 1], sc[:, LNC_T1:LNC_T1 + 1],
                                -1.0 / DM)
            v.drain()
            v.wait_ge(sqsem, sq_val)
            v.reduce_sum(sc[:, LNC_S2:LNC_S2 + 1], junk[:, :], axis=Axis.X)
            v.drain().then_inc(jfree, 1)
            v.tensor_scalar_mul(sc[:, LNC_T1:LNC_T1 + 1], sc[:, LNC_S2:LNC_S2 + 1], 1.0 / DM)
            v.drain()
            v.tensor_tensor(sc[:, LNC_M2:LNC_M2 + 1], sc[:, LNC_NM:LNC_NM + 1],
                            sc[:, LNC_NM:LNC_NM + 1], Alu.mult)
            v.drain()
            v.tensor_tensor(sc[:, LNC_VAR:LNC_VAR + 1], sc[:, LNC_T1:LNC_T1 + 1],
                            sc[:, LNC_M2:LNC_M2 + 1], Alu.subtract)
            v.drain().then_inc(varsem, 1)

        @block.sync
        def _(sync):
            sync.dma_start(out=bq_sb[:, :], in_=bq_hbm.ap()).then_inc(p1_b, 16)
            sync.dma_start(out=bk_sb[:, :], in_=bk_hbm.ap()).then_inc(p1_b, 16)
            sync.dma_start(out=idb[:, :], in_=ident_dram.ap()).then_inc(p1_id, 16)
            sync.wait_ge(p1_w, 48)
            for wi, wt in ((0, wqT), (1, wkT), (2, wvT)):
                for ch in range(NCH):
                    sync.dma_start_transpose(
                        wt[:, 128 * ch:128 * (ch + 1)],
                        wstage[:, 1024 * wi + 128 * ch:1024 * wi + 128 * (ch + 1)],
                    ).then_inc(p1_wt, 16)
            for j, (mat, ci) in enumerate(pw_jobs):
                nblk = PW_NX[mat]
                sync.wait_ge(pw_l, 16 * (j + 1))
                if j >= 1:
                    sync.wait_ge(pw_s, 16 * j)
                for blk in range(nblk):
                    sync.dma_start_transpose(
                        pwout[:, 128 * blk:128 * (blk + 1)],
                        pwin[:, 128 * blk:128 * (blk + 1)],
                    ).then_inc(pw_x, 16)
            sync.wait_ge(vch, V_TOTAL)
            for tt in range(NT4):
                p = tt % 2
                if tt >= 2:
                    sync.wait_ge(p4_r1[p], tt // 2)
                sync.dma_start(out=xch_sb[p][:, :],
                               in_=xch_hbm.ap()[128 * tt:128 * (tt + 1), :]).then_inc(p4_xch[p], 16)
            def rtx_block(kk):
                base = 4096 * (kk % 2)
                if kk == 0:
                    sync.wait_ge(p4_mm[0], NT4 // 2)
                    sync.wait_ge(p4_mm[1], NT4 // 2)
                if kk >= 2:
                    sync.wait_ge(p5_gu, 48 * (kk - 1))
                for t2 in range(4):
                    tg = 4 * kk + t2
                    pc = t2 % 2
                    sync.wait_ge(p5_cast[pc], tg // 2 + 1)
                    for ch in range(NCH):
                        sync.dma_start_transpose(
                            r1nT[:, base + P1T * ch + 128 * t2:base + P1T * ch + 128 * (t2 + 1)],
                            castst[pc][:, 128 * ch:128 * (ch + 1)],
                        ).then_inc(p5_rtx[pc], 16)

            rtx_block(0)
            for kk in range(NFB):
                if kk + 1 < NFB:
                    rtx_block(kk + 1)
                for t2 in range(4):
                    tg = 4 * kk + t2
                    p = t2 % 2
                    sync.wait_ge(p5_ln[p], tg // 2 + 1)
                    sync.dma_start(out=out_hbm.ap()[128 * tg:128 * (tg + 1), :],
                                   in_=out_sb[p][:, :]).then_inc(p5_out[p], 16)
            if debug:
                sync.wait_ge(dbg_sem, 80)
            sync.wait_ge(p5_out[0], 16 * (NT4 // 2))
            sync.wait_ge(p5_out[1], 16 * (NT4 // 2))

        @block.gpsimd
        def _(g):
            g.dma_start(out=wstage[:, 0:1024], in_=wq_hbm.ap()).then_inc(p1_w, 16)
            g.dma_start(out=wstage[:, 1024:2048], in_=wk_hbm.ap()).then_inc(p1_w, 16)
            g.dma_start(out=wstage[:, 2048:3072], in_=wv_hbm.ap()).then_inc(p1_w, 16)
            g.dma_start(out=bvb[:, :],
                        in_=bv_hbm.ap().unsqueeze(0).to_broadcast((128, 128))).then_inc(p1_bc, 16)
            g.dma_start(out=bob[:, :],
                        in_=bo_hbm.ap().unsqueeze(0).to_broadcast((128, 1024))).then_inc(p1_bc, 16)
            for tt in range(NP1):
                p = tt % 2
                if tt >= 2:
                    g.wait_ge(p1_tp[p], 4 * (tt // 2))
                for s4 in range(4):
                    g.dma_start(
                        out=xbt[p][:, 1024 * s4:1024 * (s4 + 1)],
                        in_=x_hbm.ap()[P1T * tt + 128 * s4:P1T * tt + 128 * (s4 + 1), :],
                    ).then_inc(p1_x[p], 16)
            def pw_load(j):
                mat, ci = pw_jobs[j]
                if j >= 1:
                    g.wait_ge(pw_x, 16 * _pwx_before[j])
                if mat == "wg":
                    g.dma_start(out=pwin[:, 0:1024],
                                in_=wg_hbm.ap()[128 * ci:128 * (ci + 1), :]).then_inc(pw_l, 16)
                elif mat == "wu":
                    g.dma_start(out=pwin[:, 0:1024],
                                in_=wu_hbm.ap()[128 * ci:128 * (ci + 1), :]).then_inc(pw_l, 16)
                elif mat == "wo":
                    g.dma_start(out=pwin[:, 0:1024],
                                in_=wo_hbm.ap()[128 * ci:128 * (ci + 1), :]).then_inc(pw_l, 16)
                else:
                    g.dma_start(out=pwin[:, :],
                                in_=wd_hbm.ap()[128 * ci:128 * (ci + 1), :]).then_inc(pw_l, 16)

            def pw_store(j):
                mat, ci = pw_jobs[j]
                nblk = PW_NX[mat]
                g.wait_ge(pw_x, 16 * (_pwx_before[j] + nblk))
                if mat == "wg":
                    dst = wgT_dram.ap()[:, :, 128 * ci:128 * (ci + 1)].transpose([1, 0, 2])
                    g.dma_start(out=dst, in_=pwout[:, 0:1024]
                                .rearrange("p (ch c) -> p ch c", ch=NCH, c=128)).then_inc(pw_s, 16)
                elif mat == "wu":
                    dst = wuT_dram.ap()[:, :, 128 * ci:128 * (ci + 1)].transpose([1, 0, 2])
                    g.dma_start(out=dst, in_=pwout[:, 0:1024]
                                .rearrange("p (ch c) -> p ch c", ch=NCH, c=128)).then_inc(pw_s, 16)
                elif mat == "wo":
                    dst = woT_dram.ap()[:, :, 128 * ci:128 * (ci + 1)].transpose([1, 0, 2])
                    g.dma_start(out=dst, in_=pwout[:, 0:1024]
                                .rearrange("p (ch c) -> p ch c", ch=NCH, c=128)).then_inc(pw_s, 16)
                else:
                    dst = wdT_dram.ap()[:, :, 128 * ci:128 * (ci + 1)].transpose([1, 0, 2])
                    g.dma_start(out=dst, in_=pwout[:, :]
                                .rearrange("p (ch c) -> p ch c", ch=NHC, c=128)).then_inc(pw_s, 16)

            for j in range(len(pw_jobs)):
                pw_load(j)
                pw_store(j)
            g.wait_ge(vch, VCP[("evac", 3)])
            g.dma_start(out=cc_in.ap()[0], in_=heads[:, 0:N // 2]).then_inc(p3_ccin, 16)
            g.wait_ge(p3_ccin, 16)
            g.collective_compute(
                "AllGather", Alu.bypass,
                replica_groups=[list(range(NCORES))],
                ins=[cc_in.ap()[0].opt()],
                outs=[cc_out.ap()[0].opt()],
            ).then_inc(p3_ag, 1)
            g.wait_ge(p3_ag, 1)
            pid = nc.partition_id([Pool])
            off = g.snap(pid * (S // 2))
            for r in range(NCORES):
                g.dma_start(out=heads[:, 1024 * r:1024 * (r + 1)],
                            in_=cc_out.ap()[0, r, :, bass.ds(off, S // 2)]).then_inc(p3_hm0, 16)
            g.wait_ge(vch, V_TOTAL)
            g.dma_start(out=cc_in.ap()[1], in_=heads[:, N // 2:]).then_inc(p3_ci1, 16)
            g.wait_ge(p3_ci1, 16)
            g.collective_compute(
                "AllGather", Alu.bypass,
                replica_groups=[list(range(NCORES))],
                ins=[cc_in.ap()[1].opt()],
                outs=[cc_out.ap()[1].opt()],
            ).then_inc(p3_ag, 1)
            g.wait_ge(vch, V_TOTAL)
            g.wait_ge(sch, S_TOTAL)
            g.wait_ge(pw_s, 16 * len(pw_jobs))
            for ch in range(NCH):
                g.dma_start(out=woT[:, 1024 * ch:1024 * (ch + 1)],
                            in_=woT_dram.ap()[ch]).then_inc(p3_wol, 16)
            g.wait_ge(pw_s, 16 * len(pw_jobs))
            for hc0 in (0, 1):
                g.dma_start(out=wguT[hc0][:, 0:1024]
                            .rearrange("p (ch c) -> p ch c", ch=NCH, c=128),
                            in_=wgT_dram.ap()[:, :, 128 * hc0:128 * (hc0 + 1)]
                            .transpose([1, 0, 2])).then_inc(p5_wg[hc0], 16)
                g.dma_start(out=wguT[hc0][:, 1024:2048]
                            .rearrange("p (ch c) -> p ch c", ch=NCH, c=128),
                            in_=wuT_dram.ap()[:, :, 128 * hc0:128 * (hc0 + 1)]
                            .transpose([1, 0, 2])).then_inc(p5_wg[hc0], 16)
            g.wait_ge(p3_ag, 2)
            for r in range(NCORES):
                g.dma_start(out=heads[:, N // 2 + 1024 * r:N // 2 + 1024 * (r + 1)],
                            in_=cc_out.ap()[1, r, :, bass.ds(off, S // 2)]).then_inc(p3_hm1, 16)
            g.wait_ge(p4_mm[0], NT4 // 2)
            g.wait_ge(p4_mm[1], NT4 // 2)
            g.wait_ge(pw_s, 16 * len(pw_jobs))
            g.dma_start(out=wdT[:, :].rearrange("p (hc c) -> p hc c", hc=NHC, c=1024),
                        in_=wdT_dram.ap().transpose([1, 0, 2])).then_inc(p5_wdl, 16)
            for kk in range(NFB):
                for hc in range(NHC):
                    ph = hc % 2
                    idx = NHC * kk + hc
                    if idx < 2:
                        continue
                    g.wait_ge(p5_gu, 2 * idx - 2)
                    g.dma_start(out=wguT[ph][:, 0:1024]
                                .rearrange("p (ch c) -> p ch c", ch=NCH, c=128),
                                in_=wgT_dram.ap()[:, :, 128 * hc:128 * (hc + 1)]
                                .transpose([1, 0, 2])).then_inc(p5_wg[ph], 16)
                    g.dma_start(out=wguT[ph][:, 1024:2048]
                                .rearrange("p (ch c) -> p ch c", ch=NCH, c=128),
                                in_=wuT_dram.ap()[:, :, 128 * hc:128 * (hc + 1)]
                                .transpose([1, 0, 2])).then_inc(p5_wg[ph], 16)
            if debug:
                g.wait_ge(vch, V_TOTAL)
                g.dma_start(out=dbg["q"].ap(), in_=qc[:, :]).then_inc(dbg_sem, 16)
                g.dma_start(out=dbg["k"].ap(), in_=kc[:, :]).then_inc(dbg_sem, 16)
                g.dma_start(out=dbg["v"].ap(), in_=vc[:, :]).then_inc(dbg_sem, 16)
                g.dma_start(out=dbg["heads"].ap(), in_=heads[:, :]).then_inc(dbg_sem, 16)
                g.wait_ge(p4_fin, NT4)
                g.dma_start(out=dbg["r1n"].ap(), in_=r1n[:, :]).then_inc(dbg_sem, 16)

        @block.tensor
        def _(t):
            t.wait_ge(p1_wt, 16 * 24)
            t.wait_ge(p1_id, 16)
            for tt in range(NP1):
                p = tt % 2
                t.wait_ge(p1_x[p], 64 * (tt // 2 + 1))
                for grp in range(4):
                    gi = 4 * (tt // 2) + grp
                    if gi >= 2:
                        t.wait_ge(p1_te[p], gi - 1)
                    for ch in range(NCH):
                        mi = t.transpose(
                            tp_bank[grp % 2][:, 128 * ch:128 * (ch + 1)],
                            xbt[p][:, 1024 * grp + 128 * ch:1024 * grp + 128 * (ch + 1)],
                            idb[:, :])
                    mi.then_inc(p1_tp[p], 1)
                t.wait_ge(p1_te[p], 4 * (tt // 2) + 4)
                if tt >= 2:
                    t.wait_ge(p1_evq[p], tt // 2)
                    t.wait_ge(p1_evk[p], tt // 2)
                    t.wait_ge(p1_evv[p], 4 * (tt // 2))
                for ch in range(NCH):
                    mi = t.matmul(q_ps[p][:, :], wqT[:, 128 * ch:128 * (ch + 1)],
                                  xT[p][:, P1T * ch:P1T * (ch + 1)],
                                  start=(ch == 0), stop=(ch == NCH - 1))
                mi.then_inc(p1_mmq[p], 1)
                for ch in range(NCH):
                    mi = t.matmul(k_ps[p][:, :], wkT[:, 128 * ch:128 * (ch + 1)],
                                  xT[p][:, P1T * ch:P1T * (ch + 1)],
                                  start=(ch == 0), stop=(ch == NCH - 1))
                mi.then_inc(p1_mmk[p], 1)
                for s4 in range(4):
                    for ch in range(NCH):
                        mi = t.matmul(v_ps[p][:, 128 * s4:128 * (s4 + 1)],
                                      xT[p][:, P1T * ch + 128 * s4:P1T * ch + 128 * (s4 + 1)],
                                      wvT[:, 128 * ch:128 * (ch + 1)],
                                      start=(s4 == 0 and ch == 0), stop=(ch == NCH - 1),
                                      skip_group_check=True)
                    mi.then_inc(p1_mmv[p], 1)
            for p in range(2):
                t.wait_ge(p1_evq[p], NP1 // 2)
                t.wait_ge(p1_evk[p], NP1 // 2)
                t.wait_ge(p1_evv[p], 4 * (NP1 // 2))
            nA = 0
            nB = 0

            def scores(it):
                nonlocal nA, nB
                qb, kt = divmod(it, NKT)
                pp = it % 2
                for gidx in range(4):
                    b_lo = 2 * gidx
                    if gidx % 2 == 0:
                        psT, msem, n_use = scoresA, p2_mmA, nA
                        nA += 1
                    else:
                        psT, msem, n_use = scoresB, p2_mmB, nB
                        nB += 1
                    if n_use > 0:
                        itp = (n_use - 1) // 2
                        gp = ((n_use - 1) % 2) * 2 + (gidx % 2)
                        t.wait_ge(sch, SCP[("cp", itp, gp)])
                    for bi in range(2):
                        bb = b_lo + bi
                        for h in range(2):
                            mi = t.matmul(
                                psT[:, h * 512 + bi * 256:h * 512 + bi * 256 + 256],
                                kc[64 * h:64 * (h + 1), S * bb + KT * kt:S * bb + KT * (kt + 1)],
                                qc[64 * h:64 * (h + 1), S * bb + QB * qb:S * bb + QB * (qb + 1)],
                                start=True, stop=True, tile_position=(64 * h, 0),
                                skip_group_check=True,
                            )
                    mi.then_inc(msem, 1)

            scores(0)
            scores(1)
            for it in range(NIT):
                qb, kt = divmod(it, NKT)
                pp = it % 2
                if it + 2 < NIT:
                    scores(it + 2)
                if kt == 0 and qb > 0:
                    t.wait_ge(vch, VCP[("evac", qb - 1)])
                t.wait_ge(vch, VCP[("mul", it)])
                for bb in range(B):
                    gidx, bi = divmod(bb, 2)
                    vt = 16 * bb + kt
                    for h in range(2):
                        mi = t.matmul(
                            heads_ps[64 * h:64 * (h + 1), 256 * bb:256 * (bb + 1)],
                            vc[:, 128 * vt + 64 * h:128 * vt + 64 * (h + 1)],
                            e2_sb[:, pp * 4096 + 2048 * h + 512 * gidx + 256 * bi:
                                  pp * 4096 + 2048 * h + 512 * gidx + 256 * bi + 256],
                            start=(kt == 0 and bi == 0),
                            stop=(kt == NKT - 1),
                            tile_position=(0, 64 * h),
                            skip_group_check=True,
                        )
                mi.then_inc(p2_av, 1)
            t.wait_ge(p3_hm0, 16 * NCORES)
            t.wait_ge(p3_wol, 16 * NCH)
            t.wait_ge(sch, S_TOTAL)
            for tt in range(NT4):
                p = tt % 2
                if tt == 8:
                    t.wait_ge(p3_hm1, 16 * NCORES)
                if tt >= 2:
                    t.wait_ge(p4_r1[p], tt // 2)
                for r in range(NCORES):
                    for half in range(2):
                        mi = t.matmul(
                            wo_ps[p][:, 512 * half:512 * (half + 1)],
                            heads[:, 8192 * (tt // 8) + 1024 * r + 128 * (tt % 8):
                                  8192 * (tt // 8) + 1024 * r + 128 * (tt % 8) + 128],
                            woT[:, 1024 * r + 512 * half:1024 * r + 512 * (half + 1)],
                            start=(r == 0), stop=(r == NCORES - 1),
                            skip_group_check=True,
                        )
                mi.then_inc(p4_mm[p], 1)
            t.wait_ge(vch, V_TOTAL)

            def gu_block(kk):
                t.wait_ge(p5_rtx[0], 256 * (kk + 1))
                t.wait_ge(p5_rtx[1], 256 * (kk + 1))
                for hc in range(NHC):
                    ph = hc % 2
                    idx = NHC * kk + hc
                    t.wait_ge(p5_wg[ph], 32 * (idx // 2 + 1))
                    if idx >= 2:
                        t.wait_ge(p5_silu[ph], idx // 2)
                        t.wait_ge(p5_upev[ph], idx // 2)
                    base = 4096 * (kk % 2)
                    for ch in range(NCH):
                        mi = t.matmul(gate_ps[ph][:, :], wguT[ph][:, 128 * ch:128 * (ch + 1)],
                                      r1nT[:, base + P1T * ch:base + P1T * (ch + 1)],
                                      start=(ch == 0), stop=(ch == NCH - 1))
                    mi.then_inc(p5_gu, 1)
                    for ch in range(NCH):
                        mi = t.matmul(up_ps[ph][:, :],
                                      wguT[ph][:, 1024 + 128 * ch:1024 + 128 * (ch + 1)],
                                      r1nT[:, base + P1T * ch:base + P1T * (ch + 1)],
                                      start=(ch == 0), stop=(ch == NCH - 1))
                    mi.then_inc(p5_gu, 1)

            def down_block(kk):
                if kk == 0:
                    t.wait_ge(p5_wdl, 16)
                    t.wait_ge(p4_r1[0], NT4 // 2)
                    t.wait_ge(p4_r1[1], NT4 // 2)
                t.wait_ge(p5_h, NHC * (kk + 1))
                for t2 in range(4):
                    tg = 4 * kk + t2
                    p = t2 % 2
                    if tg >= 2:
                        t.wait_ge(p5_dnev[p], tg // 2)
                    for hc in range(NHC):
                        for half in range(2):
                            mi = t.matmul(
                                down_ps[p][:, 512 * half:512 * (half + 1)],
                                hfull[:, FBLK * hc + 128 * t2:FBLK * hc + 128 * (t2 + 1)],
                                wdT[:, 1024 * hc + 512 * half:1024 * hc + 512 * (half + 1)],
                                start=(hc == 0), stop=(hc == NHC - 1),
                                skip_group_check=True,
                            )
                    mi.then_inc(p5_dn[p], 1)

            for kk in range(NFB):
                gu_block(kk)
                down_block(kk)

        @block.scalar
        def _(s):
            s.wait_ge(p1_b, 32)
            for te in range(NP1):
                p = te % 2
                s.wait_ge(p1_mmq[p], te // 2 + 1)
                s.activation(qc[:, P1T * te:P1T * (te + 1)], q_ps[p][:, :],
                             Act.Identity, bias=bq_sb[:, :], scale=1.0).then_inc(p1_evq[p], 1)
                s.wait_ge(p1_mmk[p], te // 2 + 1)
                s.activation(kc[:, P1T * te:P1T * (te + 1)], k_ps[p][:, :],
                             Act.Identity, bias=bk_sb[:, :], scale=1.0).then_inc(p1_evk[p], 1)
            nA = 0
            nB = 0

            def s_copies(it):
                nonlocal nA, nB
                pp = it % 2
                if it >= 2:
                    s.wait_ge(vch, VCP[("sub", it - 2)])
                for gidx in range(4):
                    if gidx % 2 == 0:
                        psT, msem, n_use = scoresA, p2_mmA, nA
                        nA += 1
                    else:
                        psT, msem, n_use = scoresB, p2_mmB, nB
                        nB += 1
                    s.wait_ge(msem, n_use + 1)
                    s.copy(sc_sb[:, :].rearrange("p (a h g c) -> p a h g c",
                                                 a=2, h=2, g=4, c=512)[:, pp, :, gidx],
                           psT[:, :].rearrange("p (h c) -> p h c", h=2, c=512)).then_inc(sch, 1)

            s_copies(0)
            s_copies(1)
            for it in range(NIT):
                pp = it % 2
                if it >= 2:
                    s.wait_ge(p2_av, it - 1)
                s.wait_ge(vch, VCP[("sub", it)])
                s.activation(e2_sb[:, pp * 4096:pp * 4096 + 4096],
                             e_sb[:, pp * 4096:pp * 4096 + 4096], Act.Exp).then_inc(sch, 1)
                if it + 2 < NIT:
                    s_copies(it + 2)
                s.wait_ge(vch, VCP[("zl3", it)])
                if it >= 2:
                    s.wait_ge(vch, VCP[("mul", it - 2)])
                s.activation(zbuf[:, 512 * pp:512 * (pp + 1)],
                             zbuf[:, 512 * pp:512 * (pp + 1)], Act.Ln).then_inc(sch, 1)
                s.wait_ge(sch, SCP[("ln", it)])
                s.activation(rbuf[:, 512 * pp:512 * (pp + 1)],
                             zbuf[:, 512 * pp:512 * (pp + 1)], Act.Exp,
                             scale=-1.0).then_inc(sch, 1)
            s.wait_ge(misc, 1)
            for tt in range(NT4):
                p = tt % 2
                s.wait_ge(p4_r1[p], tt // 2 + 1)
                s.wait_ge(p4_jfree, tt)
                s.activation(junk[:, :], r1st[p][:, :], Act.Square).then_inc(p4_sq, 1)
                s.wait_ge(p4_var[p], tt // 2 + 1)
                if tt >= 2:
                    s.wait_ge(p4_fin, tt - 1)
                s.activation(lnsc[p][:, LNC_SD:LNC_SD + 1], lnsc[p][:, LNC_VAR:LNC_VAR + 1],
                             Act.Ln, bias=epsv[:, :], scale=1.0)
                s.drain()
                s.activation(lnsc[p][:, LNC_SD:LNC_SD + 1], lnsc[p][:, LNC_SD:LNC_SD + 1],
                             Act.Exp, scale=-0.5)
                s.drain().then_inc(p4_sd[p], 1)
            def silu_block(kk):
                for hc in range(NHC):
                    ph = hc % 2
                    idx = NHC * kk + hc
                    s.wait_ge(p5_gu, 2 * idx + 1)
                    if idx >= 2:
                        s.wait_ge(p5_h, idx - 1)
                    s.activation(silu_st[ph][:, :], gate_ps[ph][:, :], Act.Silu,
                                 scale=SC).then_inc(p5_silu[ph], 1)

            for kk in range(NFB):
                silu_block(kk)
                for t2 in range(4):
                    tg = 4 * kk + t2
                    p = t2 % 2
                    s.wait_ge(p5_dnev[p], tg // 2 + 1)
                    s.wait_ge(p5_jfree, tg)
                    s.activation(junk[:, :], r1st[p][:, :], Act.Square).then_inc(p5_sq, 1)
                    s.wait_ge(p5_var[p], tg // 2 + 1)
                    if tg >= 2:
                        s.wait_ge(p5_ln[p], tg // 2)
                    s.activation(lnsc[p][:, LNC_SD:LNC_SD + 1], lnsc[p][:, LNC_VAR:LNC_VAR + 1],
                                 Act.Ln, bias=epsv[:, :], scale=1.0)
                    s.drain()
                    s.activation(lnsc[p][:, LNC_SD:LNC_SD + 1], lnsc[p][:, LNC_SD:LNC_SD + 1],
                                 Act.Exp, scale=-0.5)
                    s.drain().then_inc(p5_sd[p], 1)

        @block.vector
        def _(v):
            v.memset(epsv[:, :], EPS)
            v.drain().then_inc(misc, 1)
            v.wait_ge(p1_bc, 32)
            for tt in range(NP1 + 1):
                if tt < NP1:
                    p = tt % 2
                    for grp in range(4):
                        gi = 4 * (tt // 2) + grp
                        v.wait_ge(p1_tp[p], gi + 1)
                        v.tensor_copy(
                            xT[p][:, :].rearrange("p (ch sp c) -> p ch sp c",
                                                  ch=8, sp=4, c=128)[:, :, grp],
                            tp_bank[grp % 2][:, :].rearrange("p (ch c) -> p ch c",
                                                             ch=8, c=128))
                        v.drain().then_inc(p1_te[p], 1)
                if tt >= 1:
                    te = tt - 1
                    p = te % 2
                    v.wait_ge(p1_mmv[p], 4 * (te // 2) + 4)
                    v.tensor_tensor(
                        vc[:, 512 * te:512 * (te + 1)].rearrange("p (s c) -> p s c", s=4),
                        v_ps[p][:, :].rearrange("p (s c) -> p s c", s=4),
                        bvb[:, :].unsqueeze(1).to_broadcast((128, 4, 128)),
                        Alu.add)
                    v.drain().then_inc(p1_evv[p], 4)
            vn = 0

            def vop(inst):
                nonlocal vn
                inst.then_inc(vch, 1)
                vn += 1

            mt_l1 = mtmp[:, 0:2048].rearrange("p (h c) -> p h c", h=2, c=1024)
            mt_l2 = mtmp[:, 2048:3072].rearrange("p (h c) -> p h c", h=2, c=512)
            mt_m = mtmp[:, 3072:3584].rearrange("p (h c) -> p h c", h=2, c=256)

            def v_maxsub(j):
                pp = j % 2
                v.wait_ge(sch, SCP[("cp", j, 3)])
                if j >= 2:
                    v.wait_ge(sch, SCP[("exp", j - 2)])
                sc_h = _slab_h(sc_sb, pp)
                v.wait_ge(vch, vn)
                vop(v.tensor_tensor(mt_l1, sc_h[:, :, 0:1024], sc_h[:, :, 1024:2048], Alu.max))
                v.wait_ge(vch, vn)
                vop(v.tensor_tensor(mt_l2, mt_l1[:, :, 0:512], mt_l1[:, :, 512:1024], Alu.max))
                v.wait_ge(vch, vn)
                vop(v.tensor_tensor(mt_m, mt_l2[:, :, 0:256], mt_l2[:, :, 256:512], Alu.max))
                v.wait_ge(vch, vn)
                vop(v.tensor_tensor(
                    _slab_hgbc(e_sb, pp), _slab_hgbc(sc_sb, pp),
                    mt_m.unsqueeze(2).to_broadcast((128, 2, 8, QB)),
                    Alu.subtract))

            v_maxsub(0)
            for it in range(NIT):
                qb, kt = divmod(it, NKT)
                pp = it % 2
                if it + 1 < NIT:
                    v_maxsub(it + 1)
                e2_h = _slab_h(e2_sb, pp)
                v.wait_ge(sch, SCP[("exp", it)])
                v.wait_ge(vch, vn)
                vop(v.tensor_tensor(mt_l1, e2_h[:, :, 0:1024], e2_h[:, :, 1024:2048], Alu.add))
                v.wait_ge(vch, vn)
                vop(v.tensor_tensor(mt_l2, mt_l1[:, :, 0:512], mt_l1[:, :, 512:1024], Alu.add))
                v.wait_ge(vch, vn)
                vop(v.tensor_tensor(
                    zbuf[:, 512 * pp:512 * (pp + 1)].rearrange("p (h c) -> p h c", h=2, c=QB),
                    mt_l2[:, :, 0:256], mt_l2[:, :, 256:512], Alu.add))
                v.wait_ge(sch, SCP[("rexp", it)])
                v.wait_ge(vch, vn)
                vop(v.tensor_tensor(
                    _slab_hgbc(e2_sb, pp), _slab_hgbc(e2_sb, pp),
                    rbuf[:, 512 * pp:512 * (pp + 1)].rearrange("p (h c) -> p h c", h=2, c=QB)
                    .unsqueeze(2).to_broadcast((128, 2, 8, QB)),
                    Alu.mult))
                if kt == NKT - 1:
                    v.wait_ge(p2_av, it + 1)
                    v.wait_ge(vch, vn)
                    vop(v.tensor_copy(
                        heads[:, :].rearrange("p (a b q) -> p a b q", a=2, b=B, q=S // 2)
                        [:, qb // 4, :, QB * (qb % 4):QB * (qb % 4 + 1)],
                        heads_ps[:, :].rearrange("p (b q) -> p b q", b=B)))
            assert vn == V_TOTAL, (vn, V_TOTAL)
            def r1n_cast1(tg):
                pc = tg % 2
                if tg >= 2:
                    v.wait_ge(p5_rtx[pc], 128 * (tg // 2))
                v.tensor_copy(castst[pc][:, :], r1n[:, 1024 * tg:1024 * (tg + 1)])
                v.drain().then_inc(p5_cast[pc], 1)

            for tt in range(NT4):
                p = tt % 2
                v.wait_ge(p4_xch[p], 16 * (tt // 2 + 1))
                v.wait_ge(p4_mm[p], tt // 2 + 1)
                v.tensor_tensor(r1st[p][:, :], xch_sb[p][:, :], bob[:, :], Alu.add)
                v.drain()
                v.scalar_tensor_tensor(r1st[p][:, :], wo_ps[p][:, :], 1.0, r1st[p][:, :],
                                       Alu.mult, Alu.add)
                v.drain().then_inc(p4_r1[p], 1)
                ln_stats(v, p, r1st[p], p4_var[p], p4_sq, tt + 1, p4_jfree)
                v.wait_ge(p4_sd[p], tt // 2 + 1)
                v.tensor_scalar(r1n[:, 1024 * tt:1024 * (tt + 1)], r1st[p][:, :],
                                lnsc[p][:, LNC_NM:LNC_NM + 1],
                                lnsc[p][:, LNC_SD:LNC_SD + 1], Alu.add, Alu.mult)
                v.drain().then_inc(p4_fin, 1)
                if tt < 2:
                    r1n_cast1(tt)
            r1n_cast1(2)
            r1n_cast1(3)

            def hf_block(kk):
                if kk > 0:
                    v.wait_ge(p5_dn[0], 2 * kk)
                    v.wait_ge(p5_dn[1], 2 * kk)
                for hc in range(NHC):
                    ph = hc % 2
                    idx = NHC * kk + hc
                    if kk + 1 < NFB and hc in (0, 6, 12, 18):
                        r1n_cast1(4 * (kk + 1) + hc // 6)
                    v.wait_ge(p5_gu, 2 * idx + 2)
                    v.tensor_copy(upbf[ph][:, :], up_ps[ph][:, :])
                    v.drain().then_inc(p5_upev[ph], 1)
                    v.wait_ge(p5_silu[ph], idx // 2 + 1)
                    v.tensor_tensor(hfull[:, FBLK * hc:FBLK * (hc + 1)], silu_st[ph][:, :],
                                    upbf[ph][:, :], Alu.mult)
                    v.drain().then_inc(p5_h, 1)

            for kk in range(NFB):
                hf_block(kk)
                for t2 in range(4):
                    tg = 4 * kk + t2
                    p = t2 % 2
                    v.wait_ge(p5_dn[p], tg // 2 + 1)
                    v.scalar_tensor_tensor(r1st[p][:, :], down_ps[p][:, :], SC * SC,
                                           r1n[:, 1024 * tg:1024 * (tg + 1)], Alu.mult, Alu.add)
                    v.drain().then_inc(p5_dnev[p], 1)
                    ln_stats(v, p, r1st[p], p5_var[p], p5_sq, tg + 1, p5_jfree)
                    v.wait_ge(p5_sd[p], tg // 2 + 1)
                    if tg >= 2:
                        v.wait_ge(p5_out[p], 16 * (tg // 2))
                    v.tensor_scalar(out_sb[p][:, :], r1st[p][:, :],
                                    lnsc[p][:, LNC_NM:LNC_NM + 1],
                                    lnsc[p][:, LNC_SD:LNC_SD + 1], Alu.add, Alu.mult)
                    v.drain().then_inc(p5_ln[p], 1)

    return nc


_cache = {}


def kernel(**inputs):
    x = np.ascontiguousarray(np.asarray(inputs["L0_attn_x"], dtype=np.float32))
    xf = x.reshape(N, DM)
    Wq = np.ascontiguousarray(np.asarray(inputs["L0_attn_Wq"], dtype=np.float32))
    Wk = np.ascontiguousarray(np.asarray(inputs["L0_attn_Wk"], dtype=np.float32))
    Wv = np.ascontiguousarray(np.asarray(inputs["L0_attn_Wv"], dtype=np.float32))
    Wo = np.ascontiguousarray(np.asarray(inputs["L0_attn_Wo"], dtype=np.float32))
    bq = np.asarray(inputs["L0_attn_bq"], dtype=np.float32)
    bk = np.asarray(inputs["L0_attn_bk"], dtype=np.float32)
    bv = np.asarray(inputs["L0_attn_bv"], dtype=np.float32)
    bo = np.ascontiguousarray(np.asarray(inputs["L0_attn_bo"], dtype=np.float32))
    wg = np.ascontiguousarray(np.asarray(inputs["L0_ffn_w_gate"], dtype=np.int32))
    wu = np.ascontiguousarray(np.asarray(inputs["L0_ffn_w_up"], dtype=np.int32))
    wd = np.ascontiguousarray(np.asarray(inputs["L0_ffn_w_down"], dtype=np.int32))

    if "nc" not in _cache:
        _cache["nc"] = build_nc()
    nc = _cache["nc"]

    in_maps = []
    for c in range(NCORES):
        J = slice(128 * c, 128 * (c + 1))
        in_maps.append({
            "xfull": xf,
            "xch": np.ascontiguousarray(xf[CH * c:CH * (c + 1)]),
            "wq_s": np.ascontiguousarray(Wq[J]),
            "wk_s": np.ascontiguousarray(Wk[J]),
            "wv_s": np.ascontiguousarray(Wv[J]),
            "bq_s": np.ascontiguousarray(bq[J].reshape(128, 1)),
            "bk_s": np.ascontiguousarray(bk[J].reshape(128, 1)),
            "bv_s": np.ascontiguousarray(bv[J]),
            "wo": Wo,
            "bo": bo,
            "wgate": wg,
            "wup": wu,
            "wdown": wd,
        })
    res = run_bass_kernel_spmd(nc, in_maps, core_ids=list(range(NCORES)))
    out = np.stack([res.results[c]["out"] for c in range(NCORES)], axis=0)
    return out.reshape(B, S, DM)
